# version 03
# baseline (speedup 1.0000x reference)
import contextlib
import numpy as np
import ml_dtypes
import concourse.bass as bass
import concourse.mybir as mybir
from concourse.bass_utils import run_bass_kernel_spmd

F32 = mybir.dt.float32
BF16 = mybir.dt.bfloat16
AF = mybir.ActivationFunctionType
ALU = mybir.AluOpType
EType = mybir.EngineType

ENGS = ("sync", "tensor", "vector", "scalar", "gpsimd")

T = 8192
TL = 1024
D = 2048
NCHUNK = 16
NCORE = 8
LN_EPS = 1e-5
DEBUG_STAGE = 99
DEBUG_USC_COPY = False
DEBUG_SKIP_EVAC = False
DEBUG_SKIP_TRANSP = False


class Builder:

    def __init__(self, nc):
        self.nc = nc
        self.q = {e: [] for e in ENGS}
        self.cnt = {e: 0 for e in ENGS}
        self.wm = {}
        self.sem = {}
        self.res = {}
        self.fams = []
        self.nwaits = 0

    def _is_fam(self, p):
        return p not in ENGS

    def _wait(self, e, p, v):
        k = (e, p)
        if self.wm.get(k, 0) >= v:
            return
        self.wm[k] = v
        self.q[e].append(lambda eng, p=p, v=v: eng.wait_ge(self.sem[p], v))
        self.nwaits += 1

    def _dep(self, e, p, v):
        if p == e:
            if e == "tensor":
                return
            self._wait(e, p, v)
            return
        if self._is_fam(p):
            v = self.cnt[p]
        self._wait(e, p, v)

    def _gather_deps(self, reads, writes):
        deps = []
        for k in reads:
            r = self.res.get(k)
            if r is not None and r[0] is not None:
                deps.append(r[0])
        for k in writes:
            r = self.res.get(k)
            if r is not None:
                if r[0] is not None:
                    deps.append(r[0])
                deps.extend(r[1])
        return deps

    def emit(self, e, fn, reads=(), writes=(), inc=1, chan=None):
        ch = chan if chan is not None else e
        if ch not in self.cnt:
            self.cnt[ch] = 0
            self.fams.append(ch)
        for (p, v) in self._gather_deps(reads, writes):
            self._dep(e, p, v)
        self.cnt[ch] += inc
        myv = self.cnt[ch]
        self.q[e].append(lambda eng, fn=fn, ch=ch, inc=inc: fn(eng).then_inc(self.sem[ch], inc))
        t = (ch, myv)
        for k in writes:
            self.res[k] = [t, []]
        for k in reads:
            r = self.res.setdefault(k, [None, []])
            r[1].append(t)
        return t

    def dma(self, e, fam, out, in_, reads=(), writes=(), **kw):
        return self.emit(e, lambda eng: eng.dma_start(out=out, in_=in_, **kw),
                         reads=reads, writes=writes, inc=16, chan=fam)

    def barrier(self):
        snap = dict(self.cnt)
        for e in ENGS:
            for p, v in snap.items():
                if p != e and v > 0:
                    self._wait(e, p, v)

    def raw(self, e, fn):
        self.q[e].append(fn)

    def finalize_stores(self, fams):
        for f in fams:
            if self.cnt.get(f, 0) > 0:
                self._wait("gpsimd", f, self.cnt[f])

    def build(self, es):
        nc = self.nc
        for ch in list(ENGS) + self.fams:
            self.sem[ch] = es.enter_context(nc.semaphore("sem_" + str(ch)))
        block = es.enter_context(nc.Block())

        def mk(e):
            def body(eng):
                for f in self.q[e]:
                    f(eng)
            return body
        block.sync(mk("sync"))
        block.tensor(mk("tensor"))
        block.vector(mk("vector"))
        block.scalar(mk("scalar"))
        block.gpsimd(mk("gpsimd"))


class ConstPool:
    def __init__(self, nc):
        self.nc = nc
        self.vals = []
        self.tile = None

    def alloc(self, es):
        self.tile = es.enter_context(self.nc.sbuf_tensor("constpool", [128, 32], F32))

    def get(self, val):
        val = float(val)
        if val not in self.vals:
            assert len(self.vals) < 32
            self.vals.append(val)
            self.nc.const_aps.register(F32, val, self.tile[:, len(self.vals) - 1:len(self.vals)])
        return val

    def host_array(self):
        arr = np.zeros((128, 32), np.float32)
        for i, v in enumerate(self.vals):
            arr[:, i] = v
        return arr


class Kern:
    def __init__(self, nc, b, cp, es):
        self.nc = nc
        self.b = b
        self.cp = cp
        self.es = es
        self.t = {}
        self.d = {}
        self.uniq = 0

    def sb(self, name, shape, dtype):
        self.uniq += 1
        h = self.es.enter_context(self.nc.sbuf_tensor("%s_u%d" % (name, self.uniq), shape, dtype))
        self.t[name] = h
        return h

    def ps(self, name, shape, dtype):
        self.uniq += 1
        h = self.es.enter_context(self.nc.psum_tensor("%s_u%d" % (name, self.uniq), shape, dtype))
        self.t[name] = h
        return h

    @contextlib.contextmanager
    def stage(self):
        inner = contextlib.ExitStack()
        outer, outer_t = self.es, self.t
        self.es = inner
        self.t = dict(outer_t)
        try:
            yield
            self.b.barrier()
        finally:
            inner.close()
            self.es = outer
            self.t = outer_t


def alloc_common(K):
    for i in range(4):
        K.ps("psA%d" % i, [128, 512], F32)
    K.ps("psB0", [128, 512], F32)
    K.ps("psB1", [128, 512], F32)
    K.ps("psC", [128, 512], F32)
    K.ps("psT", [128, 1024], BF16)
    K.sb("X", [128, NCHUNK * TL], F32)
    K.sb("ONESB", [128, 1], BF16)
    K.sb("ONESROW", [1, 128], F32)
    K.sb("IDN", [128, 128], BF16)
    K.sb("IDNF", [128, 128], F32)
    K.sb("LN_XBC", [128, 512], BF16)
    K.sb("LN_XSQ", [128, 512], BF16)
    K.sb("LN_MS", [1, 2048], F32)
    K.sb("LN_TMP", [128, 512], F32)
    b = K.b
    b.emit("vector", lambda v: v.memset(K.t["ONESB"][:], 1.0), writes=["ONESB"])
    b.emit("vector", lambda v: v.memset(K.t["ONESROW"][:], 1.0), writes=["ONESROW"])


def setup_idn(K, idn_dram, idnf_dram):
    b = K.b
    b.dma("sync", "f_idn", K.t["IDN"][:], idn_dram, writes=["IDN"])
    b.dma("sync", "f_idnf", K.t["IDNF"][:], idnf_dram, writes=["IDNF"])


def load_const_pool(K, cp_dram):
    K.b.dma("sync", "f_cp", K.cp.tile[:], cp_dram, writes=["CONST"])


def pe_fence(K, keys):
    b, t = K.b, K.t
    b.emit("tensor", lambda tt: tt.matmul(t["psC"][0:1, 0:1], t["ONESB"][:, 0:1],
                                          t["ONESB"][:, 0:1], start=True, stop=True),
           reads=["ONESB"], writes=list(keys) + ["psC"])


def emit_ln(K, t0, W, out_col0=0, xln_w=512):
    b, t, cp = K.b, K.t, K.cp
    X, XLN = t["X"], t["XLN"]
    XBC, XSQ, ms, TMP = t["LN_XBC"], t["LN_XSQ"], t["LN_MS"], t["LN_TMP"]
    psB0, psB1 = t["psB0"], t["psB1"]
    for k in range(NCHUNK):
        xs = X[:, k * TL + t0: k * TL + t0 + W]
        b.emit("vector", lambda v, xs=xs: v.tensor_copy(XBC[:, 0:W], xs),
               reads=[("X", k)], writes=["LN_XBC"])
        b.emit("vector", lambda v: v.tensor_tensor(XSQ[:, 0:W], XBC[:, 0:W], XBC[:, 0:W], ALU.mult),
               reads=["LN_XBC"], writes=["LN_XSQ"])
        b.emit("tensor", lambda tt, k=k: tt.matmul(psB0[0:1, 0:W], t["ONESB"][:, 0:1], XBC[:, 0:W],
                                                   start=(k == 0), stop=(k == NCHUNK - 1)),
               reads=["LN_XBC", "ONESB"], writes=["psB0"])
        b.emit("tensor", lambda tt, k=k: tt.matmul(psB1[0:1, 0:W], t["ONESB"][:, 0:1], XSQ[:, 0:W],
                                                   start=(k == 0), stop=(k == NCHUNK - 1)),
               reads=["LN_XSQ", "ONESB"], writes=["psB1"])
    inv = 1.0 / D
    b.emit("vector", lambda v: v.tensor_scalar(ms[0:1, 0:W], psB0[0:1, 0:W], inv, None, ALU.mult),
           reads=["psB0"], writes=["LN_mean"])
    b.emit("vector", lambda v: v.tensor_scalar(ms[0:1, 512:512 + W], psB1[0:1, 0:W], inv, None, ALU.mult),
           reads=["psB1"], writes=["LN_ex2"])
    b.emit("vector", lambda v: v.tensor_tensor(ms[0:1, 1024:1024 + W], ms[0:1, 0:W], ms[0:1, 0:W], ALU.mult),
           reads=["LN_mean"], writes=["LN_m2"])
    b.emit("vector", lambda v: v.tensor_tensor(ms[0:1, 1024:1024 + W], ms[0:1, 512:512 + W],
                                               ms[0:1, 1024:1024 + W], ALU.subtract),
           reads=["LN_ex2", "LN_m2"], writes=["LN_var"])
    eps = cp.get(LN_EPS)
    b.emit("scalar", lambda s: s.activation(ms[0:1, 1536:1536 + W], ms[0:1, 1024:1024 + W], AF.Ln, bias=eps),
           reads=["LN_var", "CONST"], writes=["LN_lnv"])
    b.emit("scalar", lambda s: s.activation(ms[0:1, 1536:1536 + W], ms[0:1, 1536:1536 + W], AF.Exp, scale=-0.5),
           reads=["LN_lnv"], writes=["LN_rstd"])
    b.emit("tensor", lambda tt: tt.matmul(psB0[:, 0:W], t["ONESROW"][0:1, :], ms[0:1, 0:W],
                                          start=True, stop=True),
           reads=["LN_mean", "ONESROW"], writes=["psB0"])
    b.emit("tensor", lambda tt: tt.matmul(psB1[:, 0:W], t["ONESROW"][0:1, :], ms[0:1, 1536:1536 + W],
                                          start=True, stop=True),
           reads=["LN_rstd", "ONESROW"], writes=["psB1"])
    for k in range(NCHUNK):
        xs = X[:, k * TL + t0: k * TL + t0 + W]
        b.emit("vector", lambda v, xs=xs: v.tensor_tensor(TMP[:, 0:W], xs, psB0[:, 0:W], ALU.subtract),
               reads=[("X", k), "psB0"], writes=["LN_TMP"])
        dst = XLN[:, k * xln_w + out_col0: k * xln_w + out_col0 + W]
        b.emit("vector", lambda v, dst=dst: v.tensor_tensor(dst, TMP[:, 0:W], psB1[:, 0:W], ALU.mult),
               reads=["LN_TMP", "psB1"], writes=[("XLN", k)])


def alloc_ffn(K, max_inter):
    K.sb("XLN", [128, NCHUNK * 512], BF16)
    K.sb("WGB", [128, 2 * 4096], BF16)
    K.sb("WUB", [128, 2 * 4096], BF16)
    K.sb("DWB", [128, 4 * 2048], BF16)
    K.sb("HTM", [128, 4 * 512], BF16)
    K.sb("H1", [128, 4 * 512], BF16)
    K.sb("OA", [128, 4 * 2048], F32)
    K.sb("FF_S1", [128, 512], F32)
    K.sb("FF_T1", [128, 128], F32)
    K.sb("SP_SQ", [128, 512], BF16)
    K.sb("SP_SC", [128, 5 * 128], BF16)
    K.sb("SP_MK", [128, 128], BF16)


def alloc_ffn0_scales(K):
    K.sb("GSB", [128, 5632], BF16)
    K.sb("USC", [128, 44], F32)
    K.sb("DSH", [128, 16], F32)


def emit_ffn0_scales(K, gs_dram, us_dram, ds_dram):
    b, t = K.b, K.t
    psB0 = t["psB0"]
    row = t["OA"][0:1, 0:5632]
    ch = t["OA"][0:128, 6000:6128]
    b.dma("sync", "f_row", row, gs_dram, writes=["SC_ROW"])
    for j in range(11):
        b.emit("tensor", lambda tt, j=j: tt.matmul(psB0[:, 0:512], t["ONESROW"][0:1, :],
                                                   row[0:1, j * 512:(j + 1) * 512], start=True, stop=True),
               reads=["SC_ROW", "ONESROW"], writes=["psB0"])
        b.emit("vector", lambda v, j=j: v.tensor_copy(t["GSB"][:, j * 512:(j + 1) * 512], psB0[:, 0:512]),
               reads=["psB0"], writes=["GSB"])
    b.dma("sync", "f_ch", ch[0:44, :], us_dram, writes=["SC_CH"])
    b.emit("tensor", lambda tt: tt.transpose(psB0[:, 0:128], ch, t["IDNF"][:]),
           reads=["SC_CH", "IDNF"], writes=["psB0"])
    pe_fence(K, ["psB0"])
    b.emit("vector", lambda v: v.tensor_copy(t["USC"][:], psB0[:, 0:44]),
           reads=["psB0"], writes=["USC"])
    b.dma("sync", "f_ch", ch[0:16, :], ds_dram, writes=["SC_CH"])
    b.emit("tensor", lambda tt: tt.transpose(psB0[:, 0:128], ch, t["IDNF"][:]),
           reads=["SC_CH", "IDNF"], writes=["psB0"])
    pe_fence(K, ["psB0"])
    b.emit("scalar", lambda s: s.mul(t["DSH"][:], psB0[:, 0:16], 0.5),
           reads=["psB0"], writes=["DSH"])
    b.barrier()


def emit_sparsify(K, HT, col0, W):
    b, t = K.b, K.t
    SQ, SC, MK = t["SP_SQ"], t["SP_SC"], t["SP_MK"]
    g = W // 4
    hview = HT[:, col0:col0 + W].rearrange("p (g f) -> p g f", f=4)
    sqv = SQ[:, 0:W].rearrange("p (g f) -> p g f", f=4)
    b.emit("vector", lambda v: v.tensor_tensor(SQ[:, 0:W], HT[:, col0:col0 + W], HT[:, col0:col0 + W], ALU.mult),
           reads=["SP_in"], writes=["SP_SQ"])
    b.emit("vector", lambda v: v.tensor_tensor(SC[:, 0 * g:1 * g], sqv[:, :, 0], sqv[:, :, 1], ALU.max),
           reads=["SP_SQ"], writes=[("SP_SC", 0)])
    b.emit("vector", lambda v: v.tensor_tensor(SC[:, 1 * g:2 * g], sqv[:, :, 0], sqv[:, :, 1], ALU.min),
           reads=["SP_SQ"], writes=[("SP_SC", 1)])
    b.emit("vector", lambda v: v.tensor_tensor(SC[:, 2 * g:3 * g], sqv[:, :, 2], sqv[:, :, 3], ALU.max),
           reads=["SP_SQ"], writes=[("SP_SC", 2)])
    b.emit("vector", lambda v: v.tensor_tensor(SC[:, 3 * g:4 * g], sqv[:, :, 2], sqv[:, :, 3], ALU.min),
           reads=["SP_SQ"], writes=[("SP_SC", 3)])
    b.emit("vector", lambda v: v.tensor_tensor(SC[:, 0 * g:1 * g], SC[:, 0 * g:1 * g], SC[:, 2 * g:3 * g], ALU.min),
           reads=[("SP_SC", 0), ("SP_SC", 2)], writes=[("SP_SC", 0)])
    b.emit("vector", lambda v: v.tensor_tensor(SC[:, 1 * g:2 * g], SC[:, 1 * g:2 * g], SC[:, 3 * g:4 * g], ALU.max),
           reads=[("SP_SC", 1), ("SP_SC", 3)], writes=[("SP_SC", 1)])
    b.emit("vector", lambda v: v.tensor_tensor(SC[:, 4 * g:5 * g], SC[:, 0 * g:1 * g], SC[:, 1 * g:2 * g], ALU.max),
           reads=[("SP_SC", 0), ("SP_SC", 1)], writes=[("SP_SC", 4)])
    for j in range(4):
        b.emit("vector", lambda v, j=j: v.tensor_tensor(MK[:, 0:g], sqv[:, :, j], SC[:, 4 * g:5 * g], ALU.is_ge),
               reads=["SP_SQ", ("SP_SC", 4)], writes=["SP_MK"])
        b.emit("vector", lambda v, j=j: v.tensor_tensor(hview[:, :, j], hview[:, :, j], MK[:, 0:g], ALU.mult),
               reads=["SP_MK", "SP_in"], writes=["SP_in"])


def emit_ffn(K, layer, half, wcfg):
    b, t = K.b, K.t
    INTER = 5632 if layer == 0 else 8192
    t0 = half * 512
    NSB = INTER // 512

    emit_ln(K, t0, 512)

    XLN, WGB, WUB, DWB = t["XLN"], t["WGB"], t["WUB"], t["DWB"]
    HTM, H1, OA = t["HTM"], t["H1"], t["OA"]
    psB0, psB1, psT = t["psB0"], t["psB1"], t["psT"]
    psAs = [t["psA%d" % i] for i in range(4)]
    S1 = t["FF_S1"]
    wg, wu, dw = wcfg["wg"], wcfg["wu"], wcfg["dw"]

    for sb in range(NSB):
        for ib in range(2):
            ig = 512 * sb + 256 * ib
            par = (sb * 2 + ib) % 2
            wcol = par * 4096
            for q4 in range(4):
                src_g = wg[q4 * 512:(q4 + 1) * 512, ig:ig + 256].rearrange("(a p) i -> p a i", p=128)
                dst_g = WGB[:, wcol + q4 * 1024: wcol + (q4 + 1) * 1024].rearrange("p (a i) -> p a i", i=256)
                b.dma("gpsimd", "wb%d" % par, dst_g, src_g, writes=[("WGB", par, q4)])
                src_u = wu[q4 * 512:(q4 + 1) * 512, ig:ig + 256].rearrange("(a p) i -> p a i", p=128)
                dst_u = WUB[:, wcol + q4 * 1024: wcol + (q4 + 1) * 1024].rearrange("p (a i) -> p a i", i=256)
                b.dma("gpsimd", "wb%d" % par, dst_u, src_u, writes=[("WUB", par, q4)])
            for ts in range(4):
                for (WB, wkey, PS, pkey) in ((WGB, "WGB", psB0, "psB0"), (WUB, "WUB", psB1, "psB1")):
                    for kc in range(NCHUNK):
                        lhsT = XLN[:, kc * 512 + ts * 128: kc * 512 + (ts + 1) * 128]
                        rhs = WB[:, wcol + kc * 256: wcol + (kc + 1) * 256]
                        b.emit("tensor",
                               lambda tt, lhsT=lhsT, rhs=rhs, PS=PS, kc=kc:
                               tt.matmul(PS[:, 0:256], lhsT, rhs,
                                         start=(kc == 0), stop=(kc == NCHUNK - 1)),
                               reads=[("XLN", kc), (wkey, par, kc // 4)],
                               writes=[pkey])
                hcol = ts * 512 + ib * 256
                if layer == 0:
                    GSB = t["GSB"]
                    b.emit("vector", lambda v, ig=ig: v.tensor_tensor(S1[:, 0:256], psB0[:, 0:256],
                                                                      GSB[:, ig:ig + 256], ALU.mult),
                           reads=["psB0", "GSB"], writes=["FF_S1a"])
                    b.emit("scalar", lambda s: s.activation(S1[:, 256:512], S1[:, 0:256], AF.Silu),
                           reads=["FF_S1a"], writes=["FF_S1b"])
                else:
                    b.emit("scalar", lambda s: s.activation(S1[:, 256:512], psB0[:, 0:256], AF.Silu),
                           reads=["psB0"], writes=["FF_S1b"])
                b.emit("vector", lambda v, hcol=hcol: v.tensor_tensor(HTM[:, hcol:hcol + 256], S1[:, 256:512],
                                                                      psB1[:, 0:256], ALU.mult),
                       reads=["FF_S1b", "psB1"], writes=["SP_in"])
        for ts in range(4):
            if layer == 1:
                emit_sparsify(K, HTM, ts * 512, 512)
            if True:
                for ic in range(4):
                    src = HTM[:, ts * 512 + ic * 128: ts * 512 + (ic + 1) * 128]
                    b.emit("tensor", lambda tt, src=src, ic=ic: tt.transpose(
                        psT[:, ic * 128:(ic + 1) * 128], src, t["IDN"][:]),
                        reads=["SP_in", "IDN"], writes=["psT"])
            pe_fence(K, ["psT"])
            if layer == 0:
                for ic in range(4):
                    dst = H1[:, 512 * ic + 128 * ts: 512 * ic + 128 * (ts + 1)]
                    b.emit("vector", lambda v, dst=dst, ic=ic, sb=sb: v.tensor_scalar(
                        dst, psT[:, ic * 128:(ic + 1) * 128], t["USC"][:, 4 * sb + ic: 4 * sb + ic + 1],
                        None, ALU.mult),
                        reads=["psT", "USC"], writes=[("H1", sb, ic)])
            else:
                for ic in range(4):
                    dst = H1[:, 512 * ic + 128 * ts: 512 * ic + 128 * (ts + 1)]
                    b.emit("vector", lambda v, dst=dst, ic=ic: v.tensor_copy(
                        dst, psT[:, ic * 128:(ic + 1) * 128]),
                        reads=["psT"], writes=[("H1", sb, ic)])
        for ic in range(4):
            src = dw[sb * 512 + ic * 128: sb * 512 + (ic + 1) * 128, :]
            b.dma("gpsimd", "dwb", DWB[:, ic * 2048:(ic + 1) * 2048], src, writes=[("DWB", ic)])
        for ts in range(4):
            for ic in range(4):
                lhsT = H1[:, 512 * ic + 128 * ts: 512 * ic + 128 * (ts + 1)]
                for hs in range(4):
                    b.emit("tensor", lambda tt, lhsT=lhsT, ic=ic, hs=hs: tt.matmul(
                        psAs[hs][:, 0:512], lhsT,
                        DWB[:, ic * 2048 + hs * 512: ic * 2048 + (hs + 1) * 512],
                        start=(ic == 0), stop=(ic == 3), skip_group_check=True),
                        reads=[("H1", sb, ic), ("DWB", ic)], writes=[("psA", hs)])
            for hs in range(4):
                oa = OA[:, ts * 2048 + hs * 512: ts * 2048 + (hs + 1) * 512]
                if sb == 0:
                    b.emit("vector", lambda v, oa=oa, hs=hs: v.tensor_copy(oa, psAs[hs][:, 0:512]),
                           reads=[("psA", hs)], writes=[("OA", ts, hs)])
                else:
                    b.emit("vector", lambda v, oa=oa, hs=hs: v.tensor_tensor(oa, oa, psAs[hs][:, 0:512], ALU.add),
                           reads=[("psA", hs), ("OA", ts, hs)], writes=[("OA", ts, hs)])
    X = t["X"]
    for ts in range(4):
        for hc in range(NCHUNK):
            src = OA[:, ts * 2048 + hc * 128: ts * 2048 + (hc + 1) * 128]
            b.emit("tensor", lambda tt, src=src: tt.transpose(psB0[:, 0:128], src, t["IDNF"][:]),
                   reads=[("OA", ts, hc // 4), "IDNF"], writes=["psB0"])
            pe_fence(K, ["psB0"])
            xs = X[:, hc * TL + t0 + ts * 128: hc * TL + t0 + (ts + 1) * 128]
            if layer == 0:
                b.emit("vector", lambda v, hc=hc: v.tensor_scalar(t["FF_T1"][:, 0:128], psB0[:, 0:128],
                                                                  t["DSH"][:, hc:hc + 1], None, ALU.mult),
                       reads=["psB0", "DSH"], writes=["FF_T1"])
            else:
                b.emit("vector", lambda v: v.tensor_scalar(t["FF_T1"][:, 0:128], psB0[:, 0:128],
                                                           0.5, None, ALU.mult),
                       reads=["psB0"], writes=["FF_T1"])
            b.emit("vector", lambda v, xs=xs: v.tensor_tensor(xs, xs, t["FF_T1"][:, 0:128], ALU.add),
                   reads=["FF_T1", ("X", hc)], writes=[("X", hc)])


def emit_load_x_fm(K, x_fm_dram):
    b = K.b
    X = K.t["X"]
    for k in range(NCHUNK):
        b.dma("sync", "xin", X[:, k * TL:(k + 1) * TL], x_fm_dram[k * 128:(k + 1) * 128, :],
              writes=[("X", k)])


def emit_store_x_fm(K, x_fm_dram):
    b = K.b
    X = K.t["X"]
    toks = []
    for k in range(NCHUNK):
        toks.append(("xout", k))
        b.dma("gpsimd", "xout", x_fm_dram[k * 128:(k + 1) * 128, :], X[:, k * TL:(k + 1) * TL],
              reads=[("X", k)], writes=[("xout", k)])
    return toks



def _host_idn():
    return np.eye(128, dtype=np.float32).astype(ml_dtypes.bfloat16)


def _host_idnf():
    return np.eye(128, dtype=np.float32)


def build_ffn_launch(layer):
    nc = bass.Bass("TRN2", target_bir_lowering=False)
    INTER = 5632 if layer == 0 else 8192
    x_in = nc.dram_tensor("x_fm", [D, TL], F32, kind="ExternalInput").ap()
    wg = nc.dram_tensor("wg", [D, INTER], F32, kind="ExternalInput").ap()
    wu = nc.dram_tensor("wu", [D, INTER], F32, kind="ExternalInput").ap()
    dw = nc.dram_tensor("dw", [INTER, D], F32, kind="ExternalInput").ap()
    if layer == 0:
        gs = nc.dram_tensor("gs", [1, INTER], F32, kind="ExternalInput").ap()
        us = nc.dram_tensor("us", [INTER // 128, 128], F32, kind="ExternalInput").ap()
        ds = nc.dram_tensor("ds", [D // 128, 128], F32, kind="ExternalInput").ap()
    idn = nc.dram_tensor("idn", [128, 128], BF16, kind="ExternalInput").ap()
    idnf = nc.dram_tensor("idnf", [128, 128], F32, kind="ExternalInput").ap()
    cpd = nc.dram_tensor("cpool", [128, 32], F32, kind="ExternalInput").ap()
    x_out = nc.dram_tensor("x_out", [D, TL], F32, kind="ExternalOutput").ap()

    es = contextlib.ExitStack()
    cp = ConstPool(nc)
    cp.alloc(es)
    b = Builder(nc)
    K = Kern(nc, b, cp, es)
    with es:
        alloc_common(K)
        alloc_ffn(K, INTER)
        if layer == 0:
            alloc_ffn0_scales(K)
        setup_idn(K, idn, idnf)
        load_const_pool(K, cpd)
        emit_load_x_fm(K, x_in)
        if layer == 0:
            emit_ffn0_scales(K, gs, us, ds)
        wcfg = {"wg": wg, "wu": wu, "dw": dw}
        for half in (0, 1):
            emit_ffn(K, layer, half, wcfg)
        emit_store_x_fm(K, x_out)
        b.finalize_stores(["xout"])
        b.build(es)
    return nc, cp




GATE_OFFSET = 6.906768
LOGD_HALF = 0.5 * float(np.log(128.0))


def alloc_attn_common(K):
    K.sb("ONESF", [128, 1], F32)
    K.b.emit("vector", lambda v: v.memset(K.t["ONESF"][:], 1.0), writes=["ONESF"])


def emit_proj_fm(K, name, w_dram, oc_n, wbuf, wfam):
    b, t = K.b, K.t
    XLN = t["XLN"]
    psBs = {"psB0": t["psB0"], "psB1": t["psB1"]}
    OUT = t[name]
    WB = t[wbuf]
    for oc in range(oc_n):
        par = oc % 2
        src = w_dram[:, oc * 128:(oc + 1) * 128].rearrange("(kc p) o -> p kc o", p=128)
        dst = WB[:, par * 2048:(par + 1) * 2048].rearrange("p (kc o) -> p kc o", o=128)
        b.dma("gpsimd", wfam + str(par), dst, src, writes=[(wbuf, par)])
        for half in range(2):
            pk = "psB0" if half == 0 else "psB1"
            PS = psBs[pk]
            for kc in range(NCHUNK):
                lhsT = WB[:, par * 2048 + kc * 128:(par * 2048) + (kc + 1) * 128]
                rhs = XLN[:, kc * 1024 + half * 512: kc * 1024 + half * 512 + 512]
                b.emit("tensor", lambda tt, lhsT=lhsT, rhs=rhs, PS=PS, kc=kc: tt.matmul(
                    PS[:, 0:512], lhsT, rhs, start=(kc == 0), stop=(kc == NCHUNK - 1)),
                    reads=[(wbuf, par), ("XLN", kc)], writes=[pk])
            dsts = OUT[:, oc * TL + half * 512: oc * TL + half * 512 + 512]
            b.emit("vector", lambda v, dsts=dsts, PS=PS: v.tensor_copy(dsts, PS[:, 0:512]),
                   reads=[pk], writes=[(name, oc)])


def emit_proj0(K, wq, wk, wv, wg):
    b, t = K.b, K.t
    emit_proj_fm(K, "QFL", wq, 16, "WPB", "wpq")
    emit_proj_fm(K, "KFL", wk, 8, "WPB", "wpk")
    emit_proj_fm(K, "VFL", wv, 8, "WPB", "wpv")
    XLN, psB0, psT = t["XLN"], t["psB0"], t["psT"]
    src = wg.rearrange("(kc p) o -> p kc o", p=128)
    b.dma("gpsimd", "wpg", t["WGQ"][:].rearrange("p (kc o) -> p kc o", o=8), src, writes=["WGQ"])
    LGTM, LGF = t["LGTM"], t["LGF"]
    cp = K.cp
    for sub in range(8):
        for kc in range(NCHUNK):
            lhsT = XLN[:, kc * 1024 + sub * 128: kc * 1024 + (sub + 1) * 128]
            b.emit("tensor", lambda tt, lhsT=lhsT, kc=kc: tt.matmul(
                psB0[:, 0:8], lhsT, t["WGQ"][:, kc * 8:(kc + 1) * 8],
                start=(kc == 0), stop=(kc == NCHUNK - 1)),
                reads=[("XLN", kc), "WGQ"], writes=["psB0"])
        b.emit("scalar", lambda s: s.activation(LGTM[:, 0:8], psB0[:, 0:8], AF.Exp,
                                                scale=-1.0, bias=cp.get(-GATE_OFFSET)),
               reads=["psB0", "CONST"], writes=["LGTM"])
        b.emit("scalar", lambda s: s.activation(LGTM[:, 8:16], LGTM[:, 0:8], AF.Ln, bias=cp.get(1.0)),
               reads=["LGTM", "CONST"], writes=["LGTM2"])
        b.emit("tensor", lambda tt: tt.transpose(psB0[0:8, 0:128], LGTM[:, 8:16], t["IDNF"][:]),
               reads=["LGTM2", "IDNF"], writes=["psB0"])
        pe_fence(K, ["psB0"])
        b.emit("vector", lambda v, sub=sub: v.tensor_copy(LGF[0:8, sub * 128:(sub + 1) * 128],
                                                          psB0[0:8, 0:128]),
               reads=["psB0"], writes=["LGF"])


def emit_proj0_out(K, qf_d, kf_d, vf_d, lg_d):
    b, t = K.b, K.t
    for oc in range(16):
        b.dma("gpsimd", "pout", qf_d[oc * 128:(oc + 1) * 128, :], t["QFL"][:, oc * TL:(oc + 1) * TL],
              reads=[("QFL", oc)], writes=[("po", "q", oc)])
    for oc in range(8):
        b.dma("gpsimd", "pout", kf_d[oc * 128:(oc + 1) * 128, :], t["KFL"][:, oc * TL:(oc + 1) * TL],
              reads=[("KFL", oc)], writes=[("po", "k", oc)])
        b.dma("gpsimd", "pout", vf_d[oc * 128:(oc + 1) * 128, :], t["VFL"][:, oc * TL:(oc + 1) * TL],
              reads=[("VFL", oc)], writes=[("po", "v", oc)])
    b.dma("gpsimd", "pout", lg_d, t["LGF"][0:8, :], reads=["LGF"], writes=[("po", "lg")])


def alloc_proj0(K):
    K.sb("XLN", [128, NCHUNK * 1024], BF16)
    K.sb("QFL", [128, 16 * TL], BF16)
    K.sb("KFL", [128, 8 * TL], BF16)
    K.sb("VFL", [128, 8 * TL], BF16)
    K.sb("WPB", [128, 2 * 2048], BF16)
    K.sb("WGQ", [128, 16 * 8], BF16)
    K.sb("LGTM", [128, 16], F32)
    K.sb("LGF", [8, TL], F32)


def build_proj0_launch():
    nc = bass.Bass("TRN2", target_bir_lowering=False)
    x_in = nc.dram_tensor("x_fm", [D, TL], F32, kind="ExternalInput").ap()
    wq = nc.dram_tensor("wq", [D, 2048], F32, kind="ExternalInput").ap()
    wk = nc.dram_tensor("wk", [D, 1024], F32, kind="ExternalInput").ap()
    wv = nc.dram_tensor("wv", [D, 1024], F32, kind="ExternalInput").ap()
    wg = nc.dram_tensor("wg", [D, 8], F32, kind="ExternalInput").ap()
    idn = nc.dram_tensor("idn", [128, 128], BF16, kind="ExternalInput").ap()
    idnf = nc.dram_tensor("idnf", [128, 128], F32, kind="ExternalInput").ap()
    cpd = nc.dram_tensor("cpool", [128, 32], F32, kind="ExternalInput").ap()
    qf = nc.dram_tensor("qf", [2048, TL], BF16, kind="ExternalOutput").ap()
    kf = nc.dram_tensor("kf", [1024, TL], BF16, kind="ExternalOutput").ap()
    vf = nc.dram_tensor("vf", [1024, TL], BF16, kind="ExternalOutput").ap()
    lg = nc.dram_tensor("lg", [8, TL], F32, kind="ExternalOutput").ap()

    es = contextlib.ExitStack()
    cp = ConstPool(nc)
    cp.alloc(es)
    b = Builder(nc)
    K = Kern(nc, b, cp, es)
    with es:
        alloc_common(K)
        alloc_proj0(K)
        setup_idn(K, idn, idnf)
        load_const_pool(K, cpd)
        emit_load_x_fm(K, x_in)
        emit_ln(K, 0, 512, out_col0=0, xln_w=1024)
        emit_ln(K, 512, 512, out_col0=512, xln_w=1024)
        emit_proj0(K, wq, wk, wv, wg)
        emit_proj0_out(K, qf, kf, vf, lg)
        b.finalize_stores(["pout"])
        b.build(es)
    return nc, cp



def alloc_attn0core(K):
    K.sb("KRES", [128, T], BF16)
    K.sb("VT", [128, 64 * 128], BF16)
    K.sb("VSTG", [128, 2 * 1024], BF16)
    K.sb("QTB", [128, 2 * 512], BF16)
    K.sb("QP", [128, 512], BF16)
    K.sb("AB", [128, 4 * 512], BF16)
    K.sb("YH", [128, 512], BF16)
    K.sb("RBS", [128, 512], F32)
    K.sb("RSB", [1, 512], F32)
    K.sb("LG0", [128, 64], F32)
    K.sb("LGC", [128, 128], F32)
    K.sb("LGPT", [128, 64], F32)
    K.sb("RFSQ", [128, 64], F32)
    K.sb("CSP", [128, 64], F32)
    K.sb("CT", [64, 128], F32)
    K.sb("OFS", [64, 1], F32)
    K.sb("CF4", [4, 128], F32)
    K.sb("TRIU", [128, 128], F32)
    K.sb("TRIS", [128, 128], F32)
    K.sb("TRIMASK", [128, 128], BF16)
    K.sb("SEL", [4, 4 * 128], F32)


def emit_lg_cumsum(K, lgh_view, off_dram):
    b, t = K.b, K.t
    psA0, psA1 = t["psA0"], t["psA1"]
    psB0 = t["psB0"]
    b.emit("vector", lambda v: v.memset(t["LGC"][:], 0.0), writes=["LGC"])
    b.emit("vector", lambda v: v.memset(t["CSP"][:], 0.0), writes=["CSP"])
    b.dma("sync", "lgh", t["LGC"][0:64, 0:128], lgh_view, writes=["LGC"])
    b.emit("tensor", lambda tt: tt.transpose(psB0[:, 0:128], t["LGC"][:, 0:128], t["IDNF"][:]),
           reads=["LGC", "IDNF"], writes=["psB0"])
    pe_fence(K, ["psB0"])
    b.emit("vector", lambda v: v.tensor_copy(t["LG0"][:], psB0[:, 0:64]),
           reads=["psB0"], writes=["LG0"])
    b.emit("tensor", lambda tt: tt.matmul(psA0[:, 0:64], t["TRIU"][:], t["LG0"][:],
                                          start=True, stop=True),
           reads=["TRIU", "LG0"], writes=[("psA", 0)])
    b.emit("vector", lambda v: v.tensor_copy(t["LGC"][:, 0:64], psA0[:, 0:64]),
           reads=[("psA", 0)], writes=["LGC"])
    b.emit("tensor", lambda tt: tt.matmul(psA1[0:1, 0:64], t["ONESF"][:, 0:1], t["LG0"][:],
                                          start=True, stop=True),
           reads=["ONESF", "LG0"], writes=[("psA", 1)])
    b.emit("vector", lambda v: v.tensor_copy(t["CSP"][0:1, :], psA1[0:1, 0:64]),
           reads=[("psA", 1)], writes=["CSP"])
    b.emit("tensor", lambda tt: tt.transpose(psB0[0:64, 0:128], t["CSP"][:, 0:64], t["IDNF"][:]),
           reads=["CSP", "IDNF"], writes=["psB0"])
    pe_fence(K, ["psB0"])
    b.emit("vector", lambda v: v.tensor_copy(t["CT"][0:64, 0:1], psB0[0:64, 0:1]),
           reads=["psB0"], writes=["CT"])
    b.emit("tensor", lambda tt: tt.matmul(psA1[0:64, 0:1], t["TRIS"][0:64, 0:64], t["CT"][0:64, 0:1],
                                          start=True, stop=True),
           reads=["TRIS", "CT"], writes=[("psA", 1)])
    b.emit("vector", lambda v: v.tensor_copy(t["OFS"][:], psA1[0:64, 0:1]),
           reads=[("psA", 1)], writes=["OFS"])
    b.dma("gpsimd", "ofs", off_dram, t["OFS"][:].rearrange("a () -> a"), reads=["OFS"], writes=["OFSD"])
    b.dma("gpsimd", "ofs", t["CSP"][0:1, :], off_dram.rearrange("a -> () a"), reads=["OFSD"], writes=["CSPR"])
    b.emit("tensor", lambda tt: tt.matmul(psB0[:, 0:64], t["ONESROW"][0:1, :], t["CSP"][0:1, 0:64],
                                          start=True, stop=True),
           reads=["CSPR", "ONESROW"], writes=["psB0"])
    b.emit("vector", lambda v: v.tensor_tensor(t["LGPT"][:], t["LGC"][:, 0:64], psB0[:, 0:64], ALU.add),
           reads=["LGC", "psB0"], writes=["LGPT"])
    b.emit("scalar", lambda s: s.activation(t["RFSQ"][:], t["LGPT"][:], AF.Exp,
                                            scale=0.5, bias=K.cp.get(-LOGD_HALF)),
           reads=["LGPT", "CONST"], writes=["RFSQ"])


def emit_v_transpose(K, vhf_fn, vt_name, nd, aug, nrows=128):
    b, t = K.b, K.t
    psT = t["psT"]
    VT, VSTG = t[vt_name], t["VSTG"]
    w = nd + (1 if aug else 0)
    if aug:
        b.emit("vector", lambda v: v.memset(VT[:], 1.0), writes=[(vt_name, "init")])
    for gblk in range(8):
        par = gblk % 2
        b.dma("sync", "vstg%d" % par, VSTG[0:nrows, par * 1024:(par * 1024) + 1024],
              vhf_fn(gblk), writes=[("VSTG", par)])
        for j in range(8):
            src = VSTG[:, par * 1024 + j * 128: par * 1024 + (j + 1) * 128]
            b.emit("tensor", lambda tt, src=src, j=j: tt.transpose(
                psT[:, j * 128:(j + 1) * 128], src, t["IDN"][:]),
                reads=[("VSTG", par), "IDN"], writes=["psT"])
        pe_fence(K, ["psT"])
        g0 = gblk * 8
        if aug:
            dst = VT[:, g0 * w: (g0 + 8) * w].rearrange("p (a q) -> p a q", q=w)[:, :, 1:1 + nd]
        else:
            dst = VT[:, g0 * w: (g0 + 8) * w].rearrange("p (a q) -> p a q", q=w)
        srcv = psT[:, 0:1024].rearrange("p (a q) -> p a q", q=128)[:, :, 0:nd]
        b.emit("vector", lambda v, dst=dst, srcv=srcv: v.tensor_copy(dst, srcv),
               reads=["psT", (vt_name, "init")] if aug else ["psT"], writes=[(vt_name, gblk)])


def emit_attn0_core(K, q_fn, y_fn):
    b, t = K.b, K.t
    psA0, psA1, psA2, psA3 = t["psA0"], t["psA1"], t["psA2"], t["psA3"]
    psB0, psB1, psC = t["psB0"], t["psB1"], t["psC"]
    QTB, QP, AB, YH, RBS, RSB = t["QTB"], t["QP"], t["AB"], t["YH"], t["RBS"], t["RSB"]
    for qt in range(16):
        b.emit("tensor", lambda tt, qt=qt: tt.transpose(psB1[0:4, 0:128],
                                                        t["LGPT"][:, 4 * qt:4 * qt + 4], t["IDNF"][:]),
               reads=["LGPT", "IDNF"], writes=["psB1"])
        pe_fence(K, ["psB1"])
        b.emit("scalar", lambda s: s.activation(t["CF4"][:], psB1[0:4, 0:128], AF.Exp, scale=-0.5),
               reads=["psB1"], writes=["CF4"])
        for j in range(4):
            b.emit("tensor", lambda tt, j=j: tt.matmul(psB0[:, j * 128:(j + 1) * 128],
                                                       t["SEL"][0:4, j * 128:(j + 1) * 128], t["CF4"][:],
                                                       start=True, stop=True, skip_group_check=True),
                   reads=["SEL", "CF4"], writes=["psB0"])
        nsc = 4 * qt + 4
        for h in range(2):
            qpar = (qt * 2 + h) % 2
            b.dma("sync", "qtl%d" % qpar, QTB[:, qpar * 512:(qpar + 1) * 512],
                  q_fn(h, qt), writes=[("QTB", qpar)])
            b.emit("vector", lambda v, qpar=qpar: v.tensor_tensor(
                QP[:], QTB[:, qpar * 512:(qpar + 1) * 512], psB0[:, 0:512], ALU.mult),
                reads=[("QTB", qpar), "psB0"], writes=["QP"])
            YPS = psA0 if h == 0 else psA1
            NPS = psA2 if h == 0 else psA3
            for sc in range(nsc):
                spar = sc % 2
                apar = sc % 4
                SP = psB1[:, 0:512] if spar == 0 else psC[:, 0:512]
                spk = "psB1" if spar == 0 else "psC"
                b.emit("tensor", lambda tt, SP=SP, sc=sc: tt.matmul(
                    SP, t["KRES"][:, sc * 128:(sc + 1) * 128], QP[:], start=True, stop=True),
                    reads=["KRES", "QP"], writes=[spk])
                abs_ = AB[:, apar * 512:(apar + 1) * 512]
                b.emit("scalar", lambda s, abs_=abs_, SP=SP, sc=sc: s.activation(
                    abs_, SP, AF.Square, scale=t["RFSQ"][:, sc:sc + 1]),
                    reads=[spk, "RFSQ"], writes=[("AB", apar)])
                if sc >= 4 * qt:
                    rel = sc - 4 * qt
                    if rel > 0:
                        b.emit("vector", lambda v, abs_=abs_, rel=rel: v.memset(abs_[:, 0:rel * 128], 0.0),
                               reads=[("AB", apar)], writes=[("AB", apar)])
                    b.emit("vector", lambda v, abs_=abs_, rel=rel: v.tensor_tensor(
                        abs_[:, rel * 128:(rel + 1) * 128], abs_[:, rel * 128:(rel + 1) * 128],
                        t["TRIMASK"][:], ALU.mult),
                        reads=[("AB", apar), "TRIMASK"], writes=[("AB", apar)])
                b.emit("tensor", lambda tt, abs_=abs_, sc=sc, YPS=YPS: tt.matmul(
                    YPS[:, 0:512], t["VT"][:, sc * 128:(sc + 1) * 128], abs_,
                    start=(sc == 0), stop=(sc == nsc - 1), skip_group_check=True),
                    reads=[("AB", apar), ("VT", sc // 8)], writes=[("psA", h)])
                b.emit("tensor", lambda tt, abs_=abs_, sc=sc, NPS=NPS: tt.matmul(
                    NPS[0:1, 0:512], t["ONESB"][:, 0:1], abs_,
                    start=(sc == 0), stop=(sc == nsc - 1), skip_group_check=True),
                    reads=[("AB", apar), "ONESB"], writes=[("psA", 2 + h)])
            b.emit("vector", lambda v, NPS=NPS: v.tensor_scalar(
                RSB[:], NPS[0:1, 0:512], 1e-6, None, ALU.add),
                reads=[("psA", 2 + h)], writes=["RSB"])
            b.emit("vector", lambda v: v.reciprocal(RSB[:], RSB[:]), reads=["RSB"], writes=["RSB"])
            b.emit("tensor", lambda tt: tt.matmul(psC[:, 0:512], t["ONESROW"][0:1, :], RSB[:],
                                                  start=True, stop=True),
                   reads=["RSB", "ONESROW"], writes=["psC"])
            b.emit("scalar", lambda s: s.copy(RBS[:], psC[:, 0:512]), reads=["psC"], writes=["RBS"])
            b.emit("vector", lambda v, YPS=YPS: v.tensor_tensor(
                YH[:], YPS[:, 0:512], RBS[:], ALU.mult),
                reads=[("psA", h), "RBS"], writes=["YH"])
            b.dma("gpsimd", "yst", y_fn(h, qt), YH[:],
                  reads=["YH"], writes=[("yst", qt, h)])


def emit_load_kres(K, kh_fn, nrows=128):
    b = K.b
    for j in range(8):
        b.dma("sync", "kres", K.t["KRES"][0:nrows, j * 1024:(j + 1) * 1024],
              kh_fn(j), writes=[("KRES", j)])
    K.b.res["KRES"] = [K.b.res[("KRES", 7)][0], []]


def _host_triu():
    return np.triu(np.ones((128, 128), np.float32))


def _host_tris():
    return np.triu(np.ones((128, 128), np.float32), 1)


def _host_trimask():
    return np.triu(np.ones((128, 128), np.float32)).astype(ml_dtypes.bfloat16)


def _host_sel():
    s = np.zeros((4, 512), np.float32)
    for j in range(4):
        s[j, j * 128:(j + 1) * 128] = 1.0
    return s


def setup_attn0_consts(K, triu_d, tris_d, trimask_d, sel_d):
    b = K.b
    b.dma("sync", "f_tri", K.t["TRIU"][:], triu_d, writes=["TRIU"])
    b.dma("sync", "f_tri", K.t["TRIS"][:], tris_d, writes=["TRIS"])
    b.dma("sync", "f_tri", K.t["TRIMASK"][:], trimask_d, writes=["TRIMASK"])
    b.dma("sync", "f_tri", K.t["SEL"][:], sel_d, writes=["SEL"])


def build_attn0core_launch():
    nc = bass.Bass("TRN2", target_bir_lowering=False)
    kh = nc.dram_tensor("kh", [128, T], BF16, kind="ExternalInput").ap()
    vhf = nc.dram_tensor("vhf", [128, T], BF16, kind="ExternalInput").ap()
    qh = nc.dram_tensor("qh", [256, T], BF16, kind="ExternalInput").ap()
    lgh = nc.dram_tensor("lgh", [64, 128], F32, kind="ExternalInput").ap()
    idn = nc.dram_tensor("idn", [128, 128], BF16, kind="ExternalInput").ap()
    idnf = nc.dram_tensor("idnf", [128, 128], F32, kind="ExternalInput").ap()
    cpd = nc.dram_tensor("cpool", [128, 32], F32, kind="ExternalInput").ap()
    triu_d = nc.dram_tensor("triu", [128, 128], F32, kind="ExternalInput").ap()
    tris_d = nc.dram_tensor("tris", [128, 128], F32, kind="ExternalInput").ap()
    trimask_d = nc.dram_tensor("trimask", [128, 128], BF16, kind="ExternalInput").ap()
    sel_d = nc.dram_tensor("sel", [4, 512], F32, kind="ExternalInput").ap()
    off_d = nc.dram_tensor("offsc", [64], F32, kind="Internal").ap()
    yh = nc.dram_tensor("yh", [256, T], BF16, kind="ExternalOutput").ap()

    es = contextlib.ExitStack()
    cp = ConstPool(nc)
    cp.alloc(es)
    b = Builder(nc)
    K = Kern(nc, b, cp, es)
    with es:
        for i in range(4):
            K.ps("psA%d" % i, [128, 512], F32)
        K.ps("psB0", [128, 512], F32)
        K.ps("psB1", [128, 512], F32)
        K.ps("psC", [128, 512], F32)
        K.ps("psT", [128, 1024], BF16)
        K.sb("ONESB", [128, 1], BF16)
        K.sb("ONESROW", [1, 128], F32)
        K.sb("IDN", [128, 128], BF16)
        K.sb("IDNF", [128, 128], F32)
        b.emit("vector", lambda v: v.memset(K.t["ONESB"][:], 1.0), writes=["ONESB"])
        b.emit("vector", lambda v: v.memset(K.t["ONESROW"][:], 1.0), writes=["ONESROW"])
        alloc_attn_common(K)
        alloc_attn0core(K)
        setup_idn(K, idn, idnf)
        load_const_pool(K, cpd)
        setup_attn0_consts(K, triu_d, tris_d, trimask_d, sel_d)
        emit_load_kres(K, lambda j: kh[:, j * 1024:(j + 1) * 1024])
        emit_v_transpose(K, lambda g: vhf[:, g * 1024:(g + 1) * 1024], "VT", 128, aug=False)
        emit_lg_cumsum(K, lgh, off_d)
        emit_attn0_core(K, lambda h, qt: qh[h * 128:(h + 1) * 128, qt * 512:(qt + 1) * 512],
                        lambda h, qt: yh[h * 128:(h + 1) * 128, qt * 512:(qt + 1) * 512])
        b.finalize_stores(["yst"])
        b.build(es)
    return nc, cp



def alloc_outproj(K):
    K.sb("YT", [128, NCHUNK * TL], BF16)
    K.sb("WCB", [128, 4 * 2048], BF16)
    K.sb("OC", [128, NCHUNK * 512], F32)
    K.sb("BCH", [128, 16], F32)
    K.sb("OP_T1", [128, 512], F32)


def emit_load_yt(K, yt_src):
    b = K.b
    for mc in range(NCHUNK):
        b.dma("sync", "ytl", K.t["YT"][:, mc * TL:(mc + 1) * TL], yt_src(mc),
              writes=[("YT", mc)])


def emit_bias_load(K, bias_dram):
    b, t = K.b, K.t
    ch = t["OC"][0:128, 0:128]
    b.emit("vector", lambda v: v.memset(ch, 0.0), writes=["OCB_STG"])
    b.dma("sync", "f_bc", ch[0:16, :], bias_dram, reads=["OCB_STG"], writes=["OCB_STG"])
    b.emit("tensor", lambda tt: tt.transpose(t["psB0"][:, 0:128], ch, t["IDNF"][:]),
           reads=["OCB_STG", "IDNF"], writes=["psB0"])
    pe_fence(K, ["psB0"])
    b.emit("vector", lambda v: v.tensor_copy(t["BCH"][:], t["psB0"][:, 0:16]),
           reads=["psB0"], writes=["BCH"])
    K.b.barrier()


def emit_out_proj(K, w_dram):
    b, t = K.b, K.t
    YT, WCB, OC = t["YT"], t["WCB"], t["OC"]
    psAs = [t["psA%d" % i] for i in range(4)]
    X = t["X"]
    for half in range(2):
        tc0 = half * 512
        for sbm in range(4):
            for j in range(4):
                mc = sbm * 4 + j
                src = w_dram[mc * 128:(mc + 1) * 128, :]
                b.dma("gpsimd", "wcb%d" % (mc % 2), WCB[:, j * 2048:(j + 1) * 2048], src,
                      writes=[("WCB", j)])
            for cc in range(NCHUNK):
                pa = cc % 4
                for j in range(4):
                    mc = sbm * 4 + j
                    lhsT = WCB[:, j * 2048 + cc * 128: j * 2048 + (cc + 1) * 128]
                    rhs = YT[:, mc * TL + tc0: mc * TL + tc0 + 512]
                    b.emit("tensor", lambda tt, lhsT=lhsT, rhs=rhs, pa=pa, j=j: tt.matmul(
                        psAs[pa][:, 0:512], lhsT, rhs, start=(j == 0), stop=(j == 3),
                        skip_group_check=True),
                        reads=[("WCB", j), ("YT", mc)], writes=[("psA", pa)])
                oc = OC[:, cc * 512:(cc + 1) * 512]
                if sbm == 0:
                    b.emit("vector", lambda v, oc=oc, pa=pa: v.tensor_copy(oc, psAs[pa][:, 0:512]),
                           reads=[("psA", pa)], writes=[("OC", cc)])
                else:
                    b.emit("vector", lambda v, oc=oc, pa=pa: v.tensor_tensor(oc, oc, psAs[pa][:, 0:512], ALU.add),
                           reads=[("psA", pa), ("OC", cc)], writes=[("OC", cc)])
        for cc in range(NCHUNK):
            oc = OC[:, cc * 512:(cc + 1) * 512]
            xs = X[:, cc * TL + tc0: cc * TL + tc0 + 512]
            b.emit("vector", lambda v, oc=oc, cc=cc: v.tensor_scalar(
                t["OP_T1"][:], oc, t["BCH"][:, cc:cc + 1], None, ALU.add),
                reads=[("OC", cc), "BCH"], writes=["OP_T1"])
            b.emit("vector", lambda v, xs=xs: v.tensor_tensor(xs, xs, t["OP_T1"][:], ALU.add),
                   reads=["OP_T1", ("X", cc)], writes=[("X", cc)])


def build_outproj_launch():
    nc = bass.Bass("TRN2", target_bir_lowering=False)
    x_in = nc.dram_tensor("x_fm", [D, TL], F32, kind="ExternalInput").ap()
    yt_in = nc.dram_tensor("yt", [D, TL], BF16, kind="ExternalInput").ap()
    w = nc.dram_tensor("w", [D, D], F32, kind="ExternalInput").ap()
    bias = nc.dram_tensor("bias", [16, 128], F32, kind="ExternalInput").ap()
    idn = nc.dram_tensor("idn", [128, 128], BF16, kind="ExternalInput").ap()
    idnf = nc.dram_tensor("idnf", [128, 128], F32, kind="ExternalInput").ap()
    cpd = nc.dram_tensor("cpool", [128, 32], F32, kind="ExternalInput").ap()
    x_out = nc.dram_tensor("x_out", [D, TL], F32, kind="ExternalOutput").ap()

    es = contextlib.ExitStack()
    cp = ConstPool(nc)
    cp.alloc(es)
    b = Builder(nc)
    K = Kern(nc, b, cp, es)
    with es:
        alloc_common(K)
        alloc_outproj(K)
        setup_idn(K, idn, idnf)
        load_const_pool(K, cpd)
        emit_load_x_fm(K, x_in)
        emit_load_yt(K, lambda mc: yt_in[mc * 128:(mc + 1) * 128, :])
        emit_bias_load(K, bias)
        emit_out_proj(K, w)
        emit_store_x_fm(K, x_out)
        b.finalize_stores(["xout"])
        b.build(es)
    return nc, cp



def alloc_proj1(K):
    K.sb("XLN", [128, NCHUNK * 1024], BF16)
    K.sb("QFL", [128, 16 * TL], BF16)
    K.sb("KFL", [128, 4 * TL], BF16)
    K.sb("VFL", [128, 4 * TL], BF16)
    K.sb("WPB", [128, 2 * 2048], BF16)
    K.sb("WTKB", [128, 8192], BF16)
    K.sb("QTM", [128, 512], F32)
    K.sb("ROT", [128, 512], F32)
    K.sb("RP1", [128, 512], F32)
    K.sb("RP2", [128, 512], F32)
    K.sb("QR", [128, 512], BF16)
    K.sb("COSR", [128, 2 * 512], F32)
    K.sb("SINR", [128, 2 * 512], F32)


def emit_proj1(K, wq, wk, wv, cos_d, sin_d):
    b, t = K.b, K.t
    XLN, psT = t["XLN"], t["psT"]
    psAs = [t["psA%d" % i] for i in range(4)]
    emit_proj_fm(K, "VFL", wv, 4, "WPB", "wpv")
    for ocb in range(4):
        src = wq[:, ocb * 512:(ocb + 1) * 512].rearrange("(kc p) o -> p kc o", p=128)
        dst = t["WTKB"][:].rearrange("p (kc o) -> p kc o", o=512)
        b.dma("gpsimd", "wq1", dst, src, writes=["WTKB"])
        for sub in range(8):
            for kc in range(NCHUNK):
                lhsT = XLN[:, kc * 1024 + sub * 128: kc * 1024 + (sub + 1) * 128]
                rhs = t["WTKB"][:, kc * 512:(kc + 1) * 512]
                b.emit("tensor", lambda tt, lhsT=lhsT, rhs=rhs, ocb=ocb, kc=kc: tt.matmul(
                    psAs[ocb % 4][:, 0:512], lhsT, rhs, start=(kc == 0), stop=(kc == NCHUNK - 1),
                    skip_group_check=True),
                    reads=["WTKB", ("XLN", kc)], writes=[("psA", ocb % 4)])
            cpar = sub % 2
            b.dma("sync", "cos%d" % cpar, t["COSR"][:, cpar * 512:(cpar + 1) * 512],
                  cos_d[sub * 128:(sub + 1) * 128, ocb * 512:(ocb + 1) * 512], writes=[("COSR", cpar)])
            b.dma("sync", "sin%d" % cpar, t["SINR"][:, cpar * 512:(cpar + 1) * 512],
                  sin_d[sub * 128:(sub + 1) * 128, ocb * 512:(ocb + 1) * 512], writes=[("SINR", cpar)])
            emit_rope_block(K, ocb, sub, cpar)
    src = wk.rearrange("(kc p) o -> p kc o", p=128)
    dst = t["WTKB"][:].rearrange("p (kc o) -> p kc o", o=512)
    b.dma("gpsimd", "wq1", dst, src, writes=["WTKB"])
    for sub in range(8):
        for kc in range(NCHUNK):
            lhsT = XLN[:, kc * 1024 + sub * 128: kc * 1024 + (sub + 1) * 128]
            rhs = t["WTKB"][:, kc * 512:(kc + 1) * 512]
            b.emit("tensor", lambda tt, lhsT=lhsT, rhs=rhs, kc=kc: tt.matmul(
                psAs[0][:, 0:512], lhsT, rhs, start=(kc == 0), stop=(kc == NCHUNK - 1),
                skip_group_check=True),
                reads=["WTKB", ("XLN", kc)], writes=[("psA", 0)])
        cpar = sub % 2
        b.dma("sync", "cos%d" % cpar, t["COSR"][:, cpar * 512:(cpar + 1) * 512],
              cos_d[sub * 128:(sub + 1) * 128, 0:512], writes=[("COSR", cpar)])
        b.dma("sync", "sin%d" % cpar, t["SINR"][:, cpar * 512:(cpar + 1) * 512],
              sin_d[sub * 128:(sub + 1) * 128, 0:512], writes=[("SINR", cpar)])
        emit_rope_k_block(K, sub, cpar)


def emit_rope_block(K, ocb, sub, cpar):
    b, t = K.b, K.t
    psT = t["psT"]
    QTM, ROT, RP1, RP2, QR = t["QTM"], t["ROT"], t["RP1"], t["RP2"], t["QR"]
    pa = ocb % 4
    co = cpar * 512
    b.emit("vector", lambda v: v.tensor_copy(QTM[:, 0:512], t["psA%d" % pa][:, 0:512]),
           reads=[("psA", pa)], writes=["QTM"])
    ev = QTM[:, 0:512].rearrange("p (g two) -> p g two", two=2)
    rv = ROT[:, 0:512].rearrange("p (g two) -> p g two", two=2)
    b.emit("vector", lambda v: v.tensor_scalar(rv[:, :, 0], ev[:, :, 1], -1.0, None, ALU.mult),
           reads=["QTM"], writes=["ROT"])
    b.emit("vector", lambda v: v.tensor_copy(rv[:, :, 1], ev[:, :, 0]),
           reads=["QTM"], writes=["ROT"])
    b.emit("vector", lambda v: v.tensor_tensor(RP1[:, 0:512], QTM[:, 0:512],
                                               t["COSR"][:, co:co + 512], ALU.mult),
           reads=["QTM", ("COSR", cpar)], writes=["RP1"])
    b.emit("vector", lambda v: v.tensor_tensor(RP2[:, 0:512], ROT[:, 0:512],
                                               t["SINR"][:, co:co + 512], ALU.mult),
           reads=["ROT", ("SINR", cpar)], writes=["RP2"])
    b.emit("vector", lambda v: v.tensor_tensor(QR[:, 0:512], RP1[:, 0:512], RP2[:, 0:512], ALU.add),
           reads=["RP1", "RP2"], writes=["QR"])
    for j in range(4):
        src = QR[:, j * 128:(j + 1) * 128]
        b.emit("tensor", lambda tt, src=src, j=j: tt.transpose(
            psT[:, j * 128:(j + 1) * 128], src, t["IDN"][:]),
            reads=["QR", "IDN"], writes=["psT"])
    pe_fence(K, ["psT"])
    QFL = t["QFL"]
    for j in range(4):
        oc = 4 * ocb + j
        dst = QFL[:, oc * TL + sub * 128: oc * TL + sub * 128 + 128]
        b.emit("vector", lambda v, dst=dst, j=j: v.tensor_copy(dst, psT[:, j * 128:(j + 1) * 128]),
               reads=["psT"], writes=[("QFL", oc)])


def emit_rope_k_block(K, sub, cpar):
    b, t = K.b, K.t
    psT = t["psT"]
    QTM, ROT, RP1, RP2, QR = t["QTM"], t["ROT"], t["RP1"], t["RP2"], t["QR"]
    co = cpar * 512
    b.emit("vector", lambda v: v.tensor_copy(QTM[:, 0:512], t["psA0"][:, 0:512]),
           reads=[("psA", 0)], writes=["QTM"])
    ev = QTM[:, 0:512].rearrange("p (g two) -> p g two", two=2)
    rv = ROT[:, 0:512].rearrange("p (g two) -> p g two", two=2)
    b.emit("vector", lambda v: v.tensor_scalar(rv[:, :, 0], ev[:, :, 1], -1.0, None, ALU.mult),
           reads=["QTM"], writes=["ROT"])
    b.emit("vector", lambda v: v.tensor_copy(rv[:, :, 1], ev[:, :, 0]),
           reads=["QTM"], writes=["ROT"])
    b.emit("vector", lambda v: v.tensor_tensor(RP1[:, 0:512], QTM[:, 0:512],
                                               t["COSR"][:, co:co + 512], ALU.mult),
           reads=["QTM", ("COSR", cpar)], writes=["RP1"])
    b.emit("vector", lambda v: v.tensor_tensor(RP2[:, 0:512], ROT[:, 0:512],
                                               t["SINR"][:, co:co + 512], ALU.mult),
           reads=["ROT", ("SINR", cpar)], writes=["RP2"])
    b.emit("vector", lambda v: v.tensor_tensor(QR[:, 0:512], RP1[:, 0:512], RP2[:, 0:512], ALU.add),
           reads=["RP1", "RP2"], writes=["QR"])
    for j in range(4):
        src = QR[:, j * 128:(j + 1) * 128]
        b.emit("tensor", lambda tt, src=src, j=j: tt.transpose(
            psT[:, j * 128:(j + 1) * 128], src, t["IDN"][:]),
            reads=["QR", "IDN"], writes=["psT"])
    pe_fence(K, ["psT"])
    KFL = t["KFL"]
    for j in range(4):
        dst = KFL[:, j * TL + sub * 128: j * TL + sub * 128 + 128]
        b.emit("vector", lambda v, dst=dst, j=j: v.tensor_copy(dst, psT[:, j * 128:(j + 1) * 128]),
               reads=["psT"], writes=[("KFL", j)])


def emit_proj1_out(K, qf_d, kf_d, vf_d):
    b, t = K.b, K.t
    for oc in range(16):
        b.dma("gpsimd", "pout", qf_d[oc * 128:(oc + 1) * 128, :], t["QFL"][:, oc * TL:(oc + 1) * TL],
              reads=[("QFL", oc)], writes=[("po", "q", oc)])
    for oc in range(4):
        b.dma("gpsimd", "pout", kf_d[oc * 128:(oc + 1) * 128, :], t["KFL"][:, oc * TL:(oc + 1) * TL],
              reads=[("KFL", oc)], writes=[("po", "k", oc)])
        b.dma("gpsimd", "pout", vf_d[oc * 128:(oc + 1) * 128, :], t["VFL"][:, oc * TL:(oc + 1) * TL],
              reads=[("VFL", oc)], writes=[("po", "v", oc)])


def build_proj1_launch():
    nc = bass.Bass("TRN2", target_bir_lowering=False)
    x_in = nc.dram_tensor("x_fm", [D, TL], F32, kind="ExternalInput").ap()
    wq = nc.dram_tensor("wq", [D, 2048], F32, kind="ExternalInput").ap()
    wk = nc.dram_tensor("wk", [D, 512], F32, kind="ExternalInput").ap()
    wv = nc.dram_tensor("wv", [D, 512], F32, kind="ExternalInput").ap()
    cos_d = nc.dram_tensor("cosr", [TL, 2048], F32, kind="ExternalInput").ap()
    sin_d = nc.dram_tensor("sinr", [TL, 2048], F32, kind="ExternalInput").ap()
    idn = nc.dram_tensor("idn", [128, 128], BF16, kind="ExternalInput").ap()
    idnf = nc.dram_tensor("idnf", [128, 128], F32, kind="ExternalInput").ap()
    cpd = nc.dram_tensor("cpool", [128, 32], F32, kind="ExternalInput").ap()
    qf = nc.dram_tensor("qf", [2048, TL], BF16, kind="ExternalOutput").ap()
    kf = nc.dram_tensor("kf", [512, TL], BF16, kind="ExternalOutput").ap()
    vf = nc.dram_tensor("vf", [512, TL], BF16, kind="ExternalOutput").ap()

    es = contextlib.ExitStack()
    cp = ConstPool(nc)
    cp.alloc(es)
    b = Builder(nc)
    K = Kern(nc, b, cp, es)
    with es:
        for i in range(4):
            K.ps("psA%d" % i, [128, 512], F32)
        K.ps("psB0", [128, 512], F32)
        K.ps("psB1", [128, 512], F32)
        K.ps("psC", [128, 512], F32)
        K.ps("psT", [128, 1024], BF16)
        K.sb("X", [128, NCHUNK * TL], F32)
        K.sb("ONESB", [128, 1], BF16)
        K.sb("ONESROW", [1, 128], F32)
        K.sb("IDN", [128, 128], BF16)
        K.sb("IDNF", [128, 128], F32)
        K.sb("LN_XBC", [128, 512], BF16)
        K.sb("LN_XSQ", [128, 512], BF16)
        K.sb("LN_MS", [1, 2048], F32)
        K.sb("LN_TMP", [128, 512], F32)
        b.emit("vector", lambda v: v.memset(K.t["ONESB"][:], 1.0), writes=["ONESB"])
        b.emit("vector", lambda v: v.memset(K.t["ONESROW"][:], 1.0), writes=["ONESROW"])
        alloc_proj1(K)
        setup_idn(K, idn, idnf)
        load_const_pool(K, cpd)
        emit_load_x_fm(K, x_in)
        emit_ln(K, 0, 512, out_col0=0, xln_w=1024)
        emit_ln(K, 512, 512, out_col0=512, xln_w=1024)
        emit_proj1(K, wq, wk, wv, cos_d, sin_d)
        emit_proj1_out(K, qf, kf, vf)
        b.finalize_stores(["pout"])
        b.build(es)
    return nc, cp


def alloc_attn1core(K):
    K.sb("KRES", [64, T], BF16)
    K.sb("VT", [128, 64 * 65], BF16)
    K.sb("VSTG", [128, 2 * 1024], BF16)
    K.sb("QTB", [64, 2 * 512], BF16)
    K.sb("AB", [128, 4 * 512], BF16)
    K.sb("ZB", [128, 512], F32)
    K.sb("TB", [128, 512], F32)
    K.sb("YH", [65, 512], BF16)
    K.sb("RBS", [65, 512], F32)
    K.sb("RSB", [1, 512], F32)
    K.sb("ALR", [1, 512], F32)
    K.sb("ALBS", [128, 512], F32)
    K.sb("SBV", [128, 4 * 64], F32)
    K.sb("ALRS", [1, 2 * 512], F32)
    K.sb("TRIMASK", [128, 128], BF16)


def emit_attn1_core_pre(K, q_fn, y_fn, alr_dram):
    b, t = K.b, K.t
    psA = [t["psA%d" % i] for i in range(4)]
    psB0, psB1, psC = t["psB0"], t["psB1"], t["psC"]
    QTB, AB, ZB, TB, YH = t["QTB"], t["AB"], t["ZB"], t["TB"], t["YH"]
    RSB, RBS, ALBS = t["RSB"], t["RBS"], t["ALBS"]
    cp = K.cp
    for qt in range(16):
        nsc = 4 * qt + 4
        for h in range(4):
            qpar = (qt * 4 + h) % 2
            b.dma("sync", "qtl%d" % qpar, QTB[:, qpar * 512:(qpar + 1) * 512],
                  q_fn(h, qt), writes=[("QTB", qpar)])
            b.dma("sync", "alr%d" % qpar, t["ALRS"][0:1, qpar * 512:(qpar + 1) * 512],
                  alr_dram[0:1, h * T + qt * 512: h * T + qt * 512 + 512],
                  writes=[("ALRS", qpar)])
            YPS = psA[h]
            for sc in range(nsc):
                spar = sc % 2
                apar = sc % 4
                SP = psB1[:, 0:512] if spar == 0 else psC[:, 0:512]
                spk = "psB1" if spar == 0 else "psC"
                b.emit("tensor", lambda tt, SP=SP, qpar=qpar: tt.matmul(
                    SP, t["ONESROW"][0:1, :],
                    t["ALRS"][0:1, qpar * 512:(qpar + 1) * 512],
                    start=True, stop=False, skip_group_check=True),
                    reads=[("ALRS", qpar), "ONESROW"], writes=[spk])
                b.emit("tensor", lambda tt, SP=SP, sc=sc, qpar=qpar: tt.matmul(
                    SP, t["KRES"][:, sc * 128:(sc + 1) * 128], QTB[:, qpar * 512:(qpar + 1) * 512],
                    start=False, stop=True, skip_group_check=True),
                    reads=["KRES", ("QTB", qpar)], writes=[spk])
                b.emit("scalar", lambda s, SP=SP, sc=sc, h=h: s.activation(
                    TB[:], SP, AF.Tanh, scale=cp.get(1.0 / 240.0),
                    bias=t["SBV"][:, h * 64 + sc: h * 64 + sc + 1]),
                    reads=[spk, ("SBV", h), "CONST"], writes=["TB"])
                abs_ = AB[:, apar * 512:(apar + 1) * 512]
                b.emit("scalar", lambda s, abs_=abs_: s.activation(
                    abs_, TB[:], AF.Exp, scale=30.0, bias=cp.get(-30.0)),
                    reads=["TB", "CONST"], writes=[("AB", apar)])
                if sc >= 4 * qt:
                    rel = sc - 4 * qt
                    if rel > 0:
                        b.emit("vector", lambda v, abs_=abs_, rel=rel: v.memset(abs_[:, 0:rel * 128], 0.0),
                               reads=[("AB", apar)], writes=[("AB", apar)])
                    b.emit("vector", lambda v, abs_=abs_, rel=rel: v.tensor_tensor(
                        abs_[:, rel * 128:(rel + 1) * 128], abs_[:, rel * 128:(rel + 1) * 128],
                        t["TRIMASK"][:], ALU.mult),
                        reads=[("AB", apar), "TRIMASK"], writes=[("AB", apar)])
                b.emit("tensor", lambda tt, abs_=abs_, sc=sc, YPS=YPS: tt.matmul(
                    YPS[0:65, 0:512], t["VT"][:, sc * 65:(sc + 1) * 65], abs_,
                    start=(sc == 0), stop=(sc == nsc - 1), skip_group_check=True),
                    reads=[("AB", apar), ("VT", sc // 8)], writes=[("psA", h)])
            b.emit("vector", lambda v, YPS=YPS: v.reciprocal(RSB[:], YPS[0:1, 0:512]),
                   reads=[("psA", h)], writes=["RSB"])
            b.emit("tensor", lambda tt: tt.matmul(psB1[0:65, 0:512], t["ONESROW"][0:1, 0:65], RSB[:],
                                                  start=True, stop=True),
                   reads=["RSB", "ONESROW"], writes=["psB1"])
            b.emit("scalar", lambda s: s.copy(RBS[0:65, :], psB1[0:65, 0:512]),
                   reads=["psB1"], writes=["RBS"])
            b.emit("vector", lambda v, YPS=YPS: v.tensor_tensor(YH[0:65, :], YPS[0:65, 0:512],
                                                                RBS[0:65, :], ALU.mult),
                   reads=[("psA", h), "RBS"], writes=["YH"])
            b.dma("gpsimd", "yst", y_fn(h, qt), YH[1:65, :],
                  reads=["YH"], writes=[("yst", qt, h)])


def emit_load_kres1(K, kh_fn):
    b = K.b
    for j in range(8):
        b.dma("sync", "kres", K.t["KRES"][0:64, j * 1024:(j + 1) * 1024],
              kh_fn(j), writes=[("KRES", j)])
    K.b.res["KRES"] = [K.b.res[("KRES", 7)][0], []]


def build_attn1core_launch():
    nc = bass.Bass("TRN2", target_bir_lowering=False)
    kh = nc.dram_tensor("kh", [64, T], BF16, kind="ExternalInput").ap()
    vhf = nc.dram_tensor("vhf", [64, T], BF16, kind="ExternalInput").ap()
    qh = nc.dram_tensor("qh", [256, T], BF16, kind="ExternalInput").ap()
    sbv_in = nc.dram_tensor("sbv", [128, 256], F32, kind="ExternalInput").ap()
    alr_in = nc.dram_tensor("alrt", [1, 4 * T], F32, kind="ExternalInput").ap()
    idn = nc.dram_tensor("idn", [128, 128], BF16, kind="ExternalInput").ap()
    idnf = nc.dram_tensor("idnf", [128, 128], F32, kind="ExternalInput").ap()
    cpd = nc.dram_tensor("cpool", [128, 32], F32, kind="ExternalInput").ap()
    trimask_d = nc.dram_tensor("trimask", [128, 128], BF16, kind="ExternalInput").ap()
    yh = nc.dram_tensor("yh", [256, T], BF16, kind="ExternalOutput").ap()
    es = contextlib.ExitStack()
    cp = ConstPool(nc)
    cp.alloc(es)
    b = Builder(nc)
    K = Kern(nc, b, cp, es)
    with es:
        for i in range(4):
            K.ps("psA%d" % i, [128, 512], F32)
        K.ps("psB0", [128, 512], F32)
        K.ps("psB1", [128, 512], F32)
        K.ps("psC", [128, 512], F32)
        K.ps("psT", [128, 1024], BF16)
        K.sb("ONESB", [128, 1], BF16)
        K.sb("ONESROW", [1, 128], F32)
        K.sb("IDN", [128, 128], BF16)
        K.sb("IDNF", [128, 128], F32)
        b.emit("vector", lambda v: v.memset(K.t["ONESB"][:], 1.0), writes=["ONESB"])
        b.emit("vector", lambda v: v.memset(K.t["ONESROW"][:], 1.0), writes=["ONESROW"])
        alloc_attn_common(K)
        alloc_attn1core(K)
        setup_idn(K, idn, idnf)
        load_const_pool(K, cpd)
        b.dma("sync", "f_tri", K.t["TRIMASK"][:], trimask_d, writes=["TRIMASK"])
        b.dma("sync", "f_sbv", K.t["SBV"][:], sbv_in, writes=[("SBV", 0), ("SBV", 1), ("SBV", 2), ("SBV", 3)])
        b.emit("vector", lambda v: v.memset(K.t["VSTG"][:], 0.0), writes=[("VSTG", 0), ("VSTG", 1)])
        emit_load_kres1(K, lambda j: kh[:, j * 1024:(j + 1) * 1024])
        emit_v_transpose(K, lambda g: vhf[:, g * 1024:(g + 1) * 1024], "VT", 64, aug=True, nrows=64)
        emit_attn1_core_pre(K, lambda h, qt: qh[h * 64:(h + 1) * 64, qt * 512:(qt + 1) * 512],
                            lambda h, qt: yh[h * 64:(h + 1) * 64, qt * 512:(qt + 1) * 512],
                            alr_in)
        b.finalize_stores(["yst"])
        b.build(es)
    return nc, cp


from concourse.aot_env import allow_ici_all_to_all


def emit_a2a(K, in_t, out_t, reads):
    def fn(g):
        with allow_ici_all_to_all():
            return g.collective_compute(
                "AllToAll", ALU.bypass,
                replica_groups=[list(range(NCORE))],
                ins=[in_t.ap().opt()], outs=[out_t.ap().opt()])
    K.b.emit("gpsimd", fn, reads=reads, writes=[("cc", out_t.name)], inc=1, chan="cc")


def build_mega_launch():
    nc = bass.Bass("TRN2", target_bir_lowering=False)

    def ei(name, shape, dt=F32):
        return nc.dram_tensor(name, shape, dt, kind="ExternalInput").ap()
    x_in = ei("x_fm", [D, TL])
    f0wg = ei("f0wg", [D, 5632]); f0wu = ei("f0wu", [D, 5632]); f0dw = ei("f0dw", [5632, D])
    f0gs = ei("f0gs", [1, 5632]); f0us = ei("f0us", [44, 128]); f0ds = ei("f0ds", [16, 128])
    a0wq = ei("a0wq", [D, 2048]); a0wk = ei("a0wk", [D, 1024]); a0wv = ei("a0wv", [D, 1024])
    a0wg = ei("a0wg", [D, 8]); a0wc = ei("a0wc", [D, D]); a0bc = ei("a0bc", [16, 128])
    f1wg = ei("f1wg", [D, 8192]); f1wu = ei("f1wu", [D, 8192]); f1dw = ei("f1dw", [8192, D])
    a1wq = ei("a1wq", [D, 2048]); a1wk = ei("a1wk", [D, 512]); a1wv = ei("a1wv", [D, 512])
    a1wo = ei("a1wo", [D, D]); a1bo = ei("a1bo", [16, 128])
    idn = ei("idn", [128, 128], BF16); idnf = ei("idnf", [128, 128])
    cpd = ei("cpool", [128, 32])
    triu_d = ei("triu", [128, 128]); tris_d = ei("tris", [128, 128])
    trimask_d = ei("trimask", [128, 128], BF16); sel_d = ei("sel", [4, 512])
    cos_d = ei("cosr", [TL, 2048]); sin_d = ei("sinr", [TL, 2048])
    sbv_in = ei("sbv", [128, 256]); alr_in = ei("alrt", [1, 4 * T])
    x_out = nc.dram_tensor("x_out", [D, TL], F32, kind="ExternalOutput").ap()

    def it(name, shape, dt):
        return nc.dram_tensor(name, shape, dt, kind="Internal")
    q0b = it("q0b", [2048, TL], BF16); q0g = it("q0g", [2048, TL], BF16)
    k0b = it("k0b", [1024, TL], BF16); k0g = it("k0g", [1024, TL], BF16)
    v0b = it("v0b", [1024, TL], BF16); v0g = it("v0g", [1024, TL], BF16)
    lg0b = it("lg0b", [8, TL], F32); lg0g = it("lg0g", [8, TL], F32)
    y0b = it("y0b", [2048, TL], BF16); y0g = it("y0g", [2048, TL], BF16)
    q1b = it("q1b", [2048, TL], BF16); q1g = it("q1g", [2048, TL], BF16)
    k1b = it("k1b", [512, TL], BF16); k1g = it("k1g", [512, TL], BF16)
    v1b = it("v1b", [512, TL], BF16); v1g = it("v1g", [512, TL], BF16)
    y1b = it("y1b", [2048, TL], BF16); y1g = it("y1g", [2048, TL], BF16)
    off_d = nc.dram_tensor("offsc", [64], F32, kind="Internal").ap()

    es = contextlib.ExitStack()
    cp = ConstPool(nc)
    cp.alloc(es)
    b = Builder(nc)
    K = Kern(nc, b, cp, es)
    with es:
        alloc_common(K)
        alloc_attn_common(K)
        setup_idn(K, idn, idnf)
        load_const_pool(K, cpd)
        emit_load_x_fm(K, x_in)
        wcfg0 = {"wg": f0wg, "wu": f0wu, "dw": f0dw}
        wcfg1 = {"wg": f1wg, "wu": f1wu, "dw": f1dw}
        with K.stage():
            alloc_ffn(K, 5632)
            alloc_ffn0_scales(K)
            emit_ffn0_scales(K, f0gs, f0us, f0ds)
            for half in (0, 1):
                emit_ffn(K, 0, half, wcfg0)
        with K.stage():
            alloc_proj0(K)
            emit_ln(K, 0, 512, out_col0=0, xln_w=1024)
            emit_ln(K, 512, 512, out_col0=512, xln_w=1024)
            emit_proj0(K, a0wq, a0wk, a0wv, a0wg)
            emit_proj0_out(K, q0b.ap(), k0b.ap(), v0b.ap(), lg0b.ap())
            emit_a2a(K, q0b, q0g, [("po", "q", oc) for oc in range(16)])
            emit_a2a(K, k0b, k0g, [("po", "k", oc) for oc in range(8)])
            emit_a2a(K, v0b, v0g, [("po", "v", oc) for oc in range(8)])
            emit_a2a(K, lg0b, lg0g, [("po", "lg")])
        with K.stage():
            alloc_attn0core(K)
            setup_attn0_consts(K, triu_d, tris_d, trimask_d, sel_d)
            emit_load_kres(K, lambda j: k0g.ap()[128 * j:128 * (j + 1), :])
            emit_v_transpose(K, lambda g: v0g.ap()[128 * g:128 * (g + 1), :], "VT", 128, aug=False)
            emit_lg_cumsum(K, lg0g.ap().rearrange("s (a u) -> (s a) u", u=128), off_d)
            emit_attn0_core(
                K,
                lambda h, qt: q0g.ap()[256 * (qt // 2) + 128 * h: 256 * (qt // 2) + 128 * (h + 1),
                                       (qt % 2) * 512:(qt % 2) * 512 + 512],
                lambda h, qt: y0b.ap()[256 * (qt // 2) + 128 * h: 256 * (qt // 2) + 128 * (h + 1),
                                       (qt % 2) * 512:(qt % 2) * 512 + 512])
            emit_a2a(K, y0b, y0g, [("yst", qt, h) for qt in range(16) for h in range(2)])
        with K.stage():
            alloc_outproj(K)
            emit_load_yt(K, lambda mc: y0g.ap()[mc * 128:(mc + 1) * 128, :])
            emit_bias_load(K, a0bc)
            emit_out_proj(K, a0wc)
        with K.stage():
            alloc_ffn(K, 5632)
            alloc_ffn0_scales(K)
            emit_ffn0_scales(K, f0gs, f0us, f0ds)
            for half in (0, 1):
                emit_ffn(K, 0, half, wcfg0)
        with K.stage():
            alloc_ffn(K, 8192)
            for half in (0, 1):
                emit_ffn(K, 1, half, wcfg1)
        with K.stage():
            alloc_proj1(K)
            emit_ln(K, 0, 512, out_col0=0, xln_w=1024)
            emit_ln(K, 512, 512, out_col0=512, xln_w=1024)
            emit_proj1(K, a1wq, a1wk, a1wv, cos_d, sin_d)
            emit_proj1_out(K, q1b.ap(), k1b.ap(), v1b.ap())
            emit_a2a(K, q1b, q1g, [("po", "q", oc) for oc in range(16)])
            emit_a2a(K, k1b, k1g, [("po", "k", oc) for oc in range(4)])
            emit_a2a(K, v1b, v1g, [("po", "v", oc) for oc in range(4)])
        with K.stage():
            alloc_attn1core(K)
            b.dma("sync", "f_tri", K.t["TRIMASK"][:], trimask_d, writes=["TRIMASK"])
            b.dma("sync", "f_sbv", K.t["SBV"][:], sbv_in,
                  writes=[("SBV", 0), ("SBV", 1), ("SBV", 2), ("SBV", 3)])
            vstg_h = K.t["VSTG"]
            b.emit("vector", lambda v: v.memset(vstg_h[:], 0.0),
                   writes=[("VSTG", 0), ("VSTG", 1)])
            emit_load_kres1(K, lambda j: k1g.ap()[64 * j:64 * (j + 1), :])
            emit_v_transpose(K, lambda g: v1g.ap()[64 * g:64 * (g + 1), :], "VT", 64,
                             aug=True, nrows=64)
            emit_attn1_core_pre(
                K,
                lambda h, qt: q1g.ap()[256 * (qt // 2) + 64 * h: 256 * (qt // 2) + 64 * (h + 1),
                                       (qt % 2) * 512:(qt % 2) * 512 + 512],
                lambda h, qt: y1b.ap()[256 * (qt // 2) + 64 * h: 256 * (qt // 2) + 64 * (h + 1),
                                       (qt % 2) * 512:(qt % 2) * 512 + 512],
                alr_in)
            emit_a2a(K, y1b, y1g, [("yst", qt, h) for qt in range(16) for h in range(4)])
        with K.stage():
            alloc_outproj(K)
            emit_load_yt(K, lambda mc: y1g.ap()[mc * 128:(mc + 1) * 128, :])
            emit_bias_load(K, a1bo)
            emit_out_proj(K, a1wo)
        with K.stage():
            alloc_ffn(K, 8192)
            for half in (0, 1):
                emit_ffn(K, 1, half, wcfg1)
        emit_store_x_fm(K, x_out)
        b.finalize_stores(["xout"])
        b.build(es)
    return nc, cp



_CACHE = {}


def _get(name, builder):
    if name not in _CACHE:
        _CACHE[name] = builder()
    return _CACHE[name]


def _rope_tables_host():
    Dh = 64
    inv = 1.0 / 10000.0 ** (np.arange(0, Dh, 2) / Dh)
    fr = np.arange(T)[:, None] * inv[None, :]
    cos = np.repeat(np.cos(fr), 2, axis=-1)
    sin = np.repeat(np.sin(fr), 2, axis=-1)
    return (np.tile(cos, (1, 32)).astype(np.float32),
            np.tile(sin, (1, 32)).astype(np.float32))


def _run(nc, in_maps):
    return run_bass_kernel_spmd(nc, in_maps, core_ids=list(range(NCORE))).results


def kernel(L0_attn_hidden_states, L0_attn_Wq, L0_attn_Wk, L0_attn_Wv, L0_attn_Wg,
           L0_attn_Wc, L0_attn_bc, L0_ffn_gate_w, L0_ffn_up_w, L0_ffn_down_w,
           L0_ffn_gate_s, L0_ffn_up_s, L0_ffn_down_s, L1_attn_Wq, L1_attn_Wk,
           L1_attn_Wv, L1_attn_Wo, L1_attn_bo, L1_ffn_w_gate, L1_ffn_w_up,
           L1_ffn_w_down):
    x0 = np.asarray(L0_attn_hidden_states)[0]
    idn, idnf = _host_idn(), _host_idnf()
    triu, tris, trimask, sel = _host_triu(), _host_tris(), _host_trimask(), _host_sel()
    cosr, sinr = _rope_tables_host()
    slopes = (2.0 ** (-(np.arange(1, 33) * 8.0 / 32))).astype(np.float64)
    giota = (np.arange(64)[None, :] * 128 + np.arange(128)[:, None]).astype(np.float32)
    tiota = np.arange(T, dtype=np.float64)
    wg0 = np.ascontiguousarray(np.asarray(L0_ffn_gate_w).T)
    wu0 = np.ascontiguousarray(np.asarray(L0_ffn_up_w).T)
    dw0 = np.ascontiguousarray(np.asarray(L0_ffn_down_w).T)
    wg1, wu1, dw1 = np.asarray(L1_ffn_w_gate), np.asarray(L1_ffn_w_up), np.asarray(L1_ffn_w_down)

    nc, cpm = _get("mega", build_mega_launch)
    cpa = cpm.host_array()
    maps = []
    for c in range(NCORE):
        sbv = np.concatenate([giota * np.float32(slopes[4 * c + h] / 240.0) for h in range(4)],
                             axis=1).astype(np.float32)
        alrt = np.concatenate([(-slopes[4 * c + h]) * tiota for h in range(4)]).reshape(1, 4 * T).astype(np.float32)
        maps.append({
            "x_fm": np.ascontiguousarray(x0[c * TL:(c + 1) * TL].T),
            "f0wg": wg0, "f0wu": wu0, "f0dw": dw0,
            "f0gs": np.asarray(L0_ffn_gate_s).reshape(1, -1),
            "f0us": np.asarray(L0_ffn_up_s).reshape(-1, 128),
            "f0ds": np.asarray(L0_ffn_down_s).reshape(-1, 128),
            "a0wq": np.asarray(L0_attn_Wq), "a0wk": np.asarray(L0_attn_Wk),
            "a0wv": np.asarray(L0_attn_Wv), "a0wg": np.asarray(L0_attn_Wg),
            "a0wc": np.asarray(L0_attn_Wc), "a0bc": np.asarray(L0_attn_bc).reshape(16, 128),
            "f1wg": wg1, "f1wu": wu1, "f1dw": dw1,
            "a1wq": np.asarray(L1_attn_Wq), "a1wk": np.asarray(L1_attn_Wk),
            "a1wv": np.asarray(L1_attn_Wv), "a1wo": np.asarray(L1_attn_Wo),
            "a1bo": np.asarray(L1_attn_bo).reshape(16, 128),
            "idn": idn, "idnf": idnf, "cpool": cpa,
            "triu": triu, "tris": tris, "trimask": trimask, "sel": sel,
            "cosr": np.ascontiguousarray(cosr[c * TL:(c + 1) * TL]),
            "sinr": np.ascontiguousarray(sinr[c * TL:(c + 1) * TL]),
            "sbv": sbv, "alrt": alrt,
        })
    r = _run(nc, maps)
    out = np.concatenate([r[c]["x_out"].T for c in range(NCORE)], axis=0)
    return out.reshape(1, T, D).astype(np.float32)


def kernel_unfused(L0_attn_hidden_states, L0_attn_Wq, L0_attn_Wk, L0_attn_Wv, L0_attn_Wg,
                   L0_attn_Wc, L0_attn_bc, L0_ffn_gate_w, L0_ffn_up_w, L0_ffn_down_w,
                   L0_ffn_gate_s, L0_ffn_up_s, L0_ffn_down_s, L1_attn_Wq, L1_attn_Wk,
                   L1_attn_Wv, L1_attn_Wo, L1_attn_bo, L1_ffn_w_gate, L1_ffn_w_up,
                   L1_ffn_w_down):
    x0 = np.asarray(L0_attn_hidden_states)[0]
    idn, idnf = _host_idn(), _host_idnf()
    triu, tris, trimask, sel = _host_triu(), _host_tris(), _host_trimask(), _host_sel()
    cosr, sinr = _rope_tables_host()
    slopes = (2.0 ** (-(np.arange(1, 33) * 8.0 / 32))).astype(np.float64)
    giota = (np.arange(64)[None, :] * 128 + np.arange(128)[:, None]).astype(np.float32)
    tiota = np.arange(T, dtype=np.float64)

    wg0 = np.ascontiguousarray(np.asarray(L0_ffn_gate_w).T)
    wu0 = np.ascontiguousarray(np.asarray(L0_ffn_up_w).T)
    dw0 = np.ascontiguousarray(np.asarray(L0_ffn_down_w).T)
    wg1, wu1, dw1 = np.asarray(L1_ffn_w_gate), np.asarray(L1_ffn_w_up), np.asarray(L1_ffn_w_down)

    nc_f0, cp_f0 = _get("ffn0", lambda: build_ffn_launch(0))
    nc_f1, cp_f1 = _get("ffn1", lambda: build_ffn_launch(1))
    nc_p0, cp_p0 = _get("proj0", build_proj0_launch)
    nc_a0, cp_a0 = _get("attn0", build_attn0core_launch)
    nc_op, cp_op = _get("outproj", build_outproj_launch)
    nc_p1, cp_p1 = _get("proj1", build_proj1_launch)
    nc_a1, cp_a1 = _get("attn1", build_attn1core_launch)

    def ffn0_maps(xs):
        return [{"x_fm": xs[c], "wg": wg0, "wu": wu0, "dw": dw0,
                 "gs": np.asarray(L0_ffn_gate_s).reshape(1, -1),
                 "us": np.asarray(L0_ffn_up_s).reshape(-1, 128),
                 "ds": np.asarray(L0_ffn_down_s).reshape(-1, 128),
                 "idn": idn, "idnf": idnf, "cpool": cp_f0.host_array()} for c in range(NCORE)]

    def ffn1_maps(xs):
        return [{"x_fm": xs[c], "wg": wg1, "wu": wu1, "dw": dw1,
                 "idn": idn, "idnf": idnf, "cpool": cp_f1.host_array()} for c in range(NCORE)]

    xs = [np.ascontiguousarray(x0[c * TL:(c + 1) * TL].T) for c in range(NCORE)]
    r = _run(nc_f0, ffn0_maps(xs))
    xs = [r[c]["x_out"] for c in range(NCORE)]
    r = _run(nc_p0, [{"x_fm": xs[c], "wq": np.asarray(L0_attn_Wq), "wk": np.asarray(L0_attn_Wk),
                      "wv": np.asarray(L0_attn_Wv), "wg": np.asarray(L0_attn_Wg),
                      "idn": idn, "idnf": idnf, "cpool": cp_p0.host_array()} for c in range(NCORE)])
    qf = np.concatenate([r[c]["qf"] for c in range(NCORE)], axis=1)
    kf = np.concatenate([r[c]["kf"] for c in range(NCORE)], axis=1)
    vf = np.concatenate([r[c]["vf"] for c in range(NCORE)], axis=1)
    lg = np.concatenate([r[c]["lg"] for c in range(NCORE)], axis=1)
    r = _run(nc_a0, [{"kh": np.ascontiguousarray(kf[128 * c:128 * (c + 1)]),
                      "vhf": np.ascontiguousarray(vf[128 * c:128 * (c + 1)]),
                      "qh": np.ascontiguousarray(qf[256 * c:256 * (c + 1)]),
                      "lgh": np.ascontiguousarray(lg[c].reshape(64, 128)),
                      "idn": idn, "idnf": idnf, "cpool": cp_a0.host_array(),
                      "triu": triu, "tris": tris, "trimask": trimask, "sel": sel}
                     for c in range(NCORE)])
    y_all = np.concatenate([r[c]["yh"] for c in range(NCORE)], axis=0)
    r = _run(nc_op, [{"x_fm": xs[c], "yt": np.ascontiguousarray(y_all[:, c * TL:(c + 1) * TL]),
                      "w": np.asarray(L0_attn_Wc), "bias": np.asarray(L0_attn_bc).reshape(16, 128),
                      "idn": idn, "idnf": idnf, "cpool": cp_op.host_array()} for c in range(NCORE)])
    xs = [r[c]["x_out"] for c in range(NCORE)]
    r = _run(nc_f0, ffn0_maps(xs))
    xs = [r[c]["x_out"] for c in range(NCORE)]
    r = _run(nc_f1, ffn1_maps(xs))
    xs = [r[c]["x_out"] for c in range(NCORE)]
    r = _run(nc_p1, [{"x_fm": xs[c], "wq": np.asarray(L1_attn_Wq), "wk": np.asarray(L1_attn_Wk),
                      "wv": np.asarray(L1_attn_Wv),
                      "cosr": np.ascontiguousarray(cosr[c * TL:(c + 1) * TL]),
                      "sinr": np.ascontiguousarray(sinr[c * TL:(c + 1) * TL]),
                      "idn": idn, "idnf": idnf, "cpool": cp_p1.host_array()} for c in range(NCORE)])
    qf = np.concatenate([r[c]["qf"] for c in range(NCORE)], axis=1)
    kf = np.concatenate([r[c]["kf"] for c in range(NCORE)], axis=1)
    vf = np.concatenate([r[c]["vf"] for c in range(NCORE)], axis=1)
    maps = []
    for c in range(NCORE):
        sbv = np.concatenate([giota * np.float32(slopes[4 * c + h] / 240.0) for h in range(4)],
                             axis=1).astype(np.float32)
        alrt = np.concatenate([(-slopes[4 * c + h]) * tiota for h in range(4)]).reshape(1, 4 * T).astype(np.float32)
        maps.append({"kh": np.ascontiguousarray(kf[64 * c:64 * (c + 1)]),
                     "vhf": np.ascontiguousarray(vf[64 * c:64 * (c + 1)]),
                     "qh": np.ascontiguousarray(qf[256 * c:256 * (c + 1)]),
                     "sbv": sbv, "alrt": alrt, "idn": idn, "idnf": idnf,
                     "cpool": cp_a1.host_array(), "trimask": trimask})
    r = _run(nc_a1, maps)
    y_all = np.concatenate([r[c]["yh"] for c in range(NCORE)], axis=0)
    r = _run(nc_op, [{"x_fm": xs[c], "yt": np.ascontiguousarray(y_all[:, c * TL:(c + 1) * TL]),
                      "w": np.asarray(L1_attn_Wo), "bias": np.asarray(L1_attn_bo).reshape(16, 128),
                      "idn": idn, "idnf": idnf, "cpool": cp_op.host_array()} for c in range(NCORE)])
    xs = [r[c]["x_out"] for c in range(NCORE)]
    r = _run(nc_f1, ffn1_maps(xs))
    xs = [r[c]["x_out"] for c in range(NCORE)]

    out = np.concatenate([xs[c].T for c in range(NCORE)], axis=0)
    return out.reshape(1, T, D).astype(np.float32)


# version 04 (final)
# speedup vs baseline: 1.0862x; 1.0862x over previous
import contextlib
import numpy as np
import ml_dtypes
import concourse.bass as bass
import concourse.mybir as mybir
from concourse.bass_utils import run_bass_kernel_spmd

F32 = mybir.dt.float32
BF16 = mybir.dt.bfloat16
AF = mybir.ActivationFunctionType
ALU = mybir.AluOpType
EType = mybir.EngineType

ENGS = ("sync", "tensor", "vector", "scalar", "gpsimd")

T = 8192
TL = 1024
D = 2048
NCHUNK = 16
NCORE = 8
LN_EPS = 1e-5
DEBUG_STAGE = 99
DEBUG_USC_COPY = False
DEBUG_SKIP_EVAC = False
DEBUG_SKIP_TRANSP = False


class Builder:

    def __init__(self, nc):
        self.nc = nc
        self.q = {e: [] for e in ENGS}
        self.cnt = {e: 0 for e in ENGS}
        self.wm = {}
        self.sem = {}
        self.res = {}
        self.fams = []
        self.nwaits = 0

    def _is_fam(self, p):
        return p not in ENGS

    def _wait(self, e, p, v):
        k = (e, p)
        if self.wm.get(k, 0) >= v:
            return
        self.wm[k] = v
        self.q[e].append(lambda eng, p=p, v=v: eng.wait_ge(self.sem[p], v))
        self.nwaits += 1

    def _dep(self, e, p, v):
        if p == e:
            if e == "tensor":
                return
            self._wait(e, p, v)
            return
        if self._is_fam(p):
            v = self.cnt[p]
        self._wait(e, p, v)

    def _gather_deps(self, reads, writes):
        deps = []
        for k in reads:
            r = self.res.get(k)
            if r is not None and r[0] is not None:
                deps.append(r[0])
        for k in writes:
            r = self.res.get(k)
            if r is not None:
                if r[0] is not None:
                    deps.append(r[0])
                deps.extend(r[1])
        return deps

    def emit(self, e, fn, reads=(), writes=(), inc=1, chan=None):
        ch = chan if chan is not None else e
        if ch not in self.cnt:
            self.cnt[ch] = 0
            self.fams.append(ch)
        for (p, v) in self._gather_deps(reads, writes):
            self._dep(e, p, v)
        self.cnt[ch] += inc
        myv = self.cnt[ch]
        self.q[e].append(lambda eng, fn=fn, ch=ch, inc=inc: fn(eng).then_inc(self.sem[ch], inc))
        t = (ch, myv)
        for k in writes:
            self.res[k] = [t, []]
        for k in reads:
            r = self.res.setdefault(k, [None, []])
            r[1].append(t)
        return t

    def dma(self, e, fam, out, in_, reads=(), writes=(), **kw):
        return self.emit(e, lambda eng: eng.dma_start(out=out, in_=in_, **kw),
                         reads=reads, writes=writes, inc=16, chan=fam)

    def barrier(self):
        snap = dict(self.cnt)
        for e in ENGS:
            for p, v in snap.items():
                if p != e and v > 0:
                    self._wait(e, p, v)

    def raw(self, e, fn):
        self.q[e].append(fn)

    def finalize_stores(self, fams):
        for f in fams:
            if self.cnt.get(f, 0) > 0:
                self._wait("gpsimd", f, self.cnt[f])

    def build(self, es):
        nc = self.nc
        for ch in list(ENGS) + self.fams:
            self.sem[ch] = es.enter_context(nc.semaphore("sem_" + str(ch)))
        block = es.enter_context(nc.Block())

        def mk(e):
            def body(eng):
                for f in self.q[e]:
                    f(eng)
            return body
        block.sync(mk("sync"))
        block.tensor(mk("tensor"))
        block.vector(mk("vector"))
        block.scalar(mk("scalar"))
        block.gpsimd(mk("gpsimd"))


class ConstPool:
    def __init__(self, nc):
        self.nc = nc
        self.vals = []
        self.tile = None

    def alloc(self, es):
        self.tile = es.enter_context(self.nc.sbuf_tensor("constpool", [128, 32], F32))

    def get(self, val):
        val = float(val)
        if val not in self.vals:
            assert len(self.vals) < 32
            self.vals.append(val)
            self.nc.const_aps.register(F32, val, self.tile[:, len(self.vals) - 1:len(self.vals)])
        return val

    def host_array(self):
        arr = np.zeros((128, 32), np.float32)
        for i, v in enumerate(self.vals):
            arr[:, i] = v
        return arr


class Kern:
    def __init__(self, nc, b, cp, es):
        self.nc = nc
        self.b = b
        self.cp = cp
        self.es = es
        self.t = {}
        self.d = {}
        self.uniq = 0

    def sb(self, name, shape, dtype):
        self.uniq += 1
        h = self.es.enter_context(self.nc.sbuf_tensor("%s_u%d" % (name, self.uniq), shape, dtype))
        self.t[name] = h
        return h

    def ps(self, name, shape, dtype):
        self.uniq += 1
        h = self.es.enter_context(self.nc.psum_tensor("%s_u%d" % (name, self.uniq), shape, dtype))
        self.t[name] = h
        return h

    @contextlib.contextmanager
    def stage(self):
        inner = contextlib.ExitStack()
        outer, outer_t = self.es, self.t
        self.es = inner
        self.t = dict(outer_t)
        try:
            yield
            self.b.barrier()
        finally:
            inner.close()
            self.es = outer
            self.t = outer_t


def alloc_common(K):
    for i in range(4):
        K.ps("psA%d" % i, [128, 512], F32)
    K.ps("psB0", [128, 512], F32)
    K.ps("psB1", [128, 512], F32)
    K.ps("psC", [128, 512], F32)
    K.ps("psT", [128, 1024], BF16)
    K.sb("X", [128, NCHUNK * TL], F32)
    K.sb("ONESB", [128, 1], BF16)
    K.sb("ONESROW", [1, 128], F32)
    K.sb("IDN", [128, 128], BF16)
    K.sb("IDNF", [128, 128], F32)
    K.sb("LN_XBC", [128, 512], BF16)
    K.sb("LN_XSQ", [128, 512], BF16)
    K.sb("LN_MS", [1, 2048], F32)
    K.sb("LN_TMP", [128, 512], F32)
    b = K.b
    b.emit("vector", lambda v: v.memset(K.t["ONESB"][:], 1.0), writes=["ONESB"])
    b.emit("vector", lambda v: v.memset(K.t["ONESROW"][:], 1.0), writes=["ONESROW"])


def setup_idn(K, idn_dram, idnf_dram):
    b = K.b
    b.dma("sync", "f_idn", K.t["IDN"][:], idn_dram, writes=["IDN"])
    b.dma("sync", "f_idnf", K.t["IDNF"][:], idnf_dram, writes=["IDNF"])


def load_const_pool(K, cp_dram):
    K.b.dma("sync", "f_cp", K.cp.tile[:], cp_dram, writes=["CONST"])


def pe_fence(K, keys):
    b, t = K.b, K.t
    b.emit("tensor", lambda tt: tt.matmul(t["psC"][0:1, 0:1], t["ONESB"][:, 0:1],
                                          t["ONESB"][:, 0:1], start=True, stop=True),
           reads=["ONESB"], writes=list(keys) + ["psC"])


def emit_ln(K, t0, W, out_col0=0, xln_w=512):
    b, t, cp = K.b, K.t, K.cp
    X, XLN = t["X"], t["XLN"]
    XBC, XSQ, ms, TMP = t["LN_XBC"], t["LN_XSQ"], t["LN_MS"], t["LN_TMP"]
    psB0, psB1 = t["psB0"], t["psB1"]
    for k in range(NCHUNK):
        xs = X[:, k * TL + t0: k * TL + t0 + W]
        b.emit("vector", lambda v, xs=xs: v.tensor_copy(XBC[:, 0:W], xs),
               reads=[("X", k)], writes=["LN_XBC"])
        b.emit("vector", lambda v: v.tensor_tensor(XSQ[:, 0:W], XBC[:, 0:W], XBC[:, 0:W], ALU.mult),
               reads=["LN_XBC"], writes=["LN_XSQ"])
        b.emit("tensor", lambda tt, k=k: tt.matmul(psB0[0:1, 0:W], t["ONESB"][:, 0:1], XBC[:, 0:W],
                                                   start=(k == 0), stop=(k == NCHUNK - 1)),
               reads=["LN_XBC", "ONESB"], writes=["psB0"])
        b.emit("tensor", lambda tt, k=k: tt.matmul(psB1[0:1, 0:W], t["ONESB"][:, 0:1], XSQ[:, 0:W],
                                                   start=(k == 0), stop=(k == NCHUNK - 1)),
               reads=["LN_XSQ", "ONESB"], writes=["psB1"])
    inv = 1.0 / D
    b.emit("vector", lambda v: v.tensor_scalar(ms[0:1, 0:W], psB0[0:1, 0:W], inv, None, ALU.mult),
           reads=["psB0"], writes=["LN_mean"])
    b.emit("vector", lambda v: v.tensor_scalar(ms[0:1, 512:512 + W], psB1[0:1, 0:W], inv, None, ALU.mult),
           reads=["psB1"], writes=["LN_ex2"])
    b.emit("vector", lambda v: v.tensor_tensor(ms[0:1, 1024:1024 + W], ms[0:1, 0:W], ms[0:1, 0:W], ALU.mult),
           reads=["LN_mean"], writes=["LN_m2"])
    b.emit("vector", lambda v: v.tensor_tensor(ms[0:1, 1024:1024 + W], ms[0:1, 512:512 + W],
                                               ms[0:1, 1024:1024 + W], ALU.subtract),
           reads=["LN_ex2", "LN_m2"], writes=["LN_var"])
    eps = cp.get(LN_EPS)
    b.emit("scalar", lambda s: s.activation(ms[0:1, 1536:1536 + W], ms[0:1, 1024:1024 + W], AF.Ln, bias=eps),
           reads=["LN_var", "CONST"], writes=["LN_lnv"])
    b.emit("scalar", lambda s: s.activation(ms[0:1, 1536:1536 + W], ms[0:1, 1536:1536 + W], AF.Exp, scale=-0.5),
           reads=["LN_lnv"], writes=["LN_rstd"])
    b.emit("tensor", lambda tt: tt.matmul(psB0[:, 0:W], t["ONESROW"][0:1, :], ms[0:1, 0:W],
                                          start=True, stop=True),
           reads=["LN_mean", "ONESROW"], writes=["psB0"])
    b.emit("tensor", lambda tt: tt.matmul(psB1[:, 0:W], t["ONESROW"][0:1, :], ms[0:1, 1536:1536 + W],
                                          start=True, stop=True),
           reads=["LN_rstd", "ONESROW"], writes=["psB1"])
    for k in range(NCHUNK):
        xs = X[:, k * TL + t0: k * TL + t0 + W]
        b.emit("vector", lambda v, xs=xs: v.tensor_tensor(TMP[:, 0:W], xs, psB0[:, 0:W], ALU.subtract),
               reads=[("X", k), "psB0"], writes=["LN_TMP"])
        dst = XLN[:, k * xln_w + out_col0: k * xln_w + out_col0 + W]
        b.emit("vector", lambda v, dst=dst: v.tensor_tensor(dst, TMP[:, 0:W], psB1[:, 0:W], ALU.mult),
               reads=["LN_TMP", "psB1"], writes=[("XLN", k)])


def alloc_ffn(K, max_inter):
    K.sb("XLN", [128, NCHUNK * 512], BF16)
    K.sb("WGB", [128, 2 * 4096], BF16)
    K.sb("WUB", [128, 2 * 4096], BF16)
    K.sb("DWB", [128, 4 * 2048], BF16)
    K.sb("HTM", [128, 4 * 512], BF16)
    K.sb("H1", [128, 4 * 512], BF16)
    K.sb("OA", [128, 4 * 2048], F32)
    K.sb("FF_S1", [128, 512], F32)
    K.sb("FF_T1", [128, 128], F32)
    K.sb("SP_SQ", [128, 512], BF16)
    K.sb("SP_SC", [128, 5 * 128], BF16)
    K.sb("SP_MK", [128, 128], BF16)


def alloc_ffn0_scales(K):
    K.sb("GSB", [128, 5632], BF16)
    K.sb("USC", [128, 44], F32)
    K.sb("DSH", [128, 16], F32)


def emit_ffn0_scales(K, gs_dram, us_dram, ds_dram):
    b, t = K.b, K.t
    psB0 = t["psB0"]
    row = t["OA"][0:1, 0:5632]
    ch = t["OA"][0:128, 6000:6128]
    b.dma("sync", "f_row", row, gs_dram, writes=["SC_ROW"])
    for j in range(11):
        b.emit("tensor", lambda tt, j=j: tt.matmul(psB0[:, 0:512], t["ONESROW"][0:1, :],
                                                   row[0:1, j * 512:(j + 1) * 512], start=True, stop=True),
               reads=["SC_ROW", "ONESROW"], writes=["psB0"])
        b.emit("vector", lambda v, j=j: v.tensor_copy(t["GSB"][:, j * 512:(j + 1) * 512], psB0[:, 0:512]),
               reads=["psB0"], writes=["GSB"])
    b.dma("sync", "f_ch", ch[0:44, :], us_dram, writes=["SC_CH"])
    b.emit("tensor", lambda tt: tt.transpose(psB0[:, 0:128], ch, t["IDNF"][:]),
           reads=["SC_CH", "IDNF"], writes=["psB0"])
    pe_fence(K, ["psB0"])
    b.emit("vector", lambda v: v.tensor_copy(t["USC"][:], psB0[:, 0:44]),
           reads=["psB0"], writes=["USC"])
    b.dma("sync", "f_ch", ch[0:16, :], ds_dram, writes=["SC_CH"])
    b.emit("tensor", lambda tt: tt.transpose(psB0[:, 0:128], ch, t["IDNF"][:]),
           reads=["SC_CH", "IDNF"], writes=["psB0"])
    pe_fence(K, ["psB0"])
    b.emit("scalar", lambda s: s.mul(t["DSH"][:], psB0[:, 0:16], 0.5),
           reads=["psB0"], writes=["DSH"])
    b.barrier()


def emit_sparsify(K, HT, col0, W):
    b, t = K.b, K.t
    SQ, SC, MK = t["SP_SQ"], t["SP_SC"], t["SP_MK"]
    g = W // 4
    hview = HT[:, col0:col0 + W].rearrange("p (g f) -> p g f", f=4)
    sqv = SQ[:, 0:W].rearrange("p (g f) -> p g f", f=4)
    b.emit("vector", lambda v: v.tensor_tensor(SQ[:, 0:W], HT[:, col0:col0 + W], HT[:, col0:col0 + W], ALU.mult),
           reads=["SP_in"], writes=["SP_SQ"])
    b.emit("vector", lambda v: v.tensor_tensor(SC[:, 0 * g:1 * g], sqv[:, :, 0], sqv[:, :, 1], ALU.max),
           reads=["SP_SQ"], writes=[("SP_SC", 0)])
    b.emit("vector", lambda v: v.tensor_tensor(SC[:, 1 * g:2 * g], sqv[:, :, 0], sqv[:, :, 1], ALU.min),
           reads=["SP_SQ"], writes=[("SP_SC", 1)])
    b.emit("vector", lambda v: v.tensor_tensor(SC[:, 2 * g:3 * g], sqv[:, :, 2], sqv[:, :, 3], ALU.max),
           reads=["SP_SQ"], writes=[("SP_SC", 2)])
    b.emit("vector", lambda v: v.tensor_tensor(SC[:, 3 * g:4 * g], sqv[:, :, 2], sqv[:, :, 3], ALU.min),
           reads=["SP_SQ"], writes=[("SP_SC", 3)])
    b.emit("vector", lambda v: v.tensor_tensor(SC[:, 0 * g:1 * g], SC[:, 0 * g:1 * g], SC[:, 2 * g:3 * g], ALU.min),
           reads=[("SP_SC", 0), ("SP_SC", 2)], writes=[("SP_SC", 0)])
    b.emit("vector", lambda v: v.tensor_tensor(SC[:, 1 * g:2 * g], SC[:, 1 * g:2 * g], SC[:, 3 * g:4 * g], ALU.max),
           reads=[("SP_SC", 1), ("SP_SC", 3)], writes=[("SP_SC", 1)])
    b.emit("vector", lambda v: v.tensor_tensor(SC[:, 4 * g:5 * g], SC[:, 0 * g:1 * g], SC[:, 1 * g:2 * g], ALU.max),
           reads=[("SP_SC", 0), ("SP_SC", 1)], writes=[("SP_SC", 4)])
    for j in range(4):
        b.emit("vector", lambda v, j=j: v.tensor_tensor(MK[:, 0:g], sqv[:, :, j], SC[:, 4 * g:5 * g], ALU.is_ge),
               reads=["SP_SQ", ("SP_SC", 4)], writes=["SP_MK"])
        b.emit("vector", lambda v, j=j: v.tensor_tensor(hview[:, :, j], hview[:, :, j], MK[:, 0:g], ALU.mult),
               reads=["SP_MK", "SP_in"], writes=["SP_in"])


def emit_ffn(K, layer, half, wcfg):
    b, t = K.b, K.t
    INTER = 5632 if layer == 0 else 8192
    t0 = half * 512
    NSB = INTER // 512

    emit_ln(K, t0, 512)

    XLN, WGB, WUB, DWB = t["XLN"], t["WGB"], t["WUB"], t["DWB"]
    HTM, H1, OA = t["HTM"], t["H1"], t["OA"]
    psB0, psB1, psT = t["psB0"], t["psB1"], t["psT"]
    psAs = [t["psA%d" % i] for i in range(4)]
    S1 = t["FF_S1"]
    wg, wu, dw = wcfg["wg"], wcfg["wu"], wcfg["dw"]

    for sb in range(NSB):
        for ib in range(2):
            ig = 512 * sb + 256 * ib
            par = (sb * 2 + ib) % 2
            wcol = par * 4096
            for q4 in range(4):
                src_g = wg[q4 * 512:(q4 + 1) * 512, ig:ig + 256].rearrange("(a p) i -> p a i", p=128)
                dst_g = WGB[:, wcol + q4 * 1024: wcol + (q4 + 1) * 1024].rearrange("p (a i) -> p a i", i=256)
                b.dma("gpsimd", "wb%d" % par, dst_g, src_g, writes=[("WGB", par, q4)])
                src_u = wu[q4 * 512:(q4 + 1) * 512, ig:ig + 256].rearrange("(a p) i -> p a i", p=128)
                dst_u = WUB[:, wcol + q4 * 1024: wcol + (q4 + 1) * 1024].rearrange("p (a i) -> p a i", i=256)
                b.dma("gpsimd", "wb%d" % par, dst_u, src_u, writes=[("WUB", par, q4)])
            for ts in range(4):
                for (WB, wkey, PS, pkey) in ((WGB, "WGB", psB0, "psB0"), (WUB, "WUB", psB1, "psB1")):
                    for kc in range(NCHUNK):
                        lhsT = XLN[:, kc * 512 + ts * 128: kc * 512 + (ts + 1) * 128]
                        rhs = WB[:, wcol + kc * 256: wcol + (kc + 1) * 256]
                        b.emit("tensor",
                               lambda tt, lhsT=lhsT, rhs=rhs, PS=PS, kc=kc:
                               tt.matmul(PS[:, 0:256], lhsT, rhs,
                                         start=(kc == 0), stop=(kc == NCHUNK - 1)),
                               reads=[("XLN", kc), (wkey, par, kc // 4)],
                               writes=[pkey])
                hcol = ts * 512 + ib * 256
                if layer == 0:
                    GSB = t["GSB"]
                    b.emit("vector", lambda v, ig=ig: v.tensor_tensor(S1[:, 0:256], psB0[:, 0:256],
                                                                      GSB[:, ig:ig + 256], ALU.mult),
                           reads=["psB0", "GSB"], writes=["FF_S1a"])
                    b.emit("scalar", lambda s: s.activation(S1[:, 256:512], S1[:, 0:256], AF.Silu),
                           reads=["FF_S1a"], writes=["FF_S1b"])
                else:
                    b.emit("scalar", lambda s: s.activation(S1[:, 256:512], psB0[:, 0:256], AF.Silu),
                           reads=["psB0"], writes=["FF_S1b"])
                b.emit("vector", lambda v, hcol=hcol: v.tensor_tensor(HTM[:, hcol:hcol + 256], S1[:, 256:512],
                                                                      psB1[:, 0:256], ALU.mult),
                       reads=["FF_S1b", "psB1"], writes=["SP_in"])
        for ts in range(4):
            if layer == 1:
                emit_sparsify(K, HTM, ts * 512, 512)
            if True:
                for ic in range(4):
                    src = HTM[:, ts * 512 + ic * 128: ts * 512 + (ic + 1) * 128]
                    b.emit("tensor", lambda tt, src=src, ic=ic: tt.transpose(
                        psT[:, ic * 128:(ic + 1) * 128], src, t["IDN"][:]),
                        reads=["SP_in", "IDN"], writes=["psT"])
            pe_fence(K, ["psT"])
            if layer == 0:
                for ic in range(4):
                    dst = H1[:, 512 * ic + 128 * ts: 512 * ic + 128 * (ts + 1)]
                    b.emit("vector", lambda v, dst=dst, ic=ic, sb=sb: v.tensor_scalar(
                        dst, psT[:, ic * 128:(ic + 1) * 128], t["USC"][:, 4 * sb + ic: 4 * sb + ic + 1],
                        None, ALU.mult),
                        reads=["psT", "USC"], writes=[("H1", sb, ic)])
            else:
                for ic in range(4):
                    dst = H1[:, 512 * ic + 128 * ts: 512 * ic + 128 * (ts + 1)]
                    b.emit("vector", lambda v, dst=dst, ic=ic: v.tensor_copy(
                        dst, psT[:, ic * 128:(ic + 1) * 128]),
                        reads=["psT"], writes=[("H1", sb, ic)])
        for ic in range(4):
            src = dw[sb * 512 + ic * 128: sb * 512 + (ic + 1) * 128, :]
            b.dma("gpsimd", "dwb", DWB[:, ic * 2048:(ic + 1) * 2048], src, writes=[("DWB", ic)])
        for ts in range(4):
            for ic in range(4):
                lhsT = H1[:, 512 * ic + 128 * ts: 512 * ic + 128 * (ts + 1)]
                for hs in range(4):
                    b.emit("tensor", lambda tt, lhsT=lhsT, ic=ic, hs=hs: tt.matmul(
                        psAs[hs][:, 0:512], lhsT,
                        DWB[:, ic * 2048 + hs * 512: ic * 2048 + (hs + 1) * 512],
                        start=(ic == 0), stop=(ic == 3), skip_group_check=True),
                        reads=[("H1", sb, ic), ("DWB", ic)], writes=[("psA", hs)])
            for hs in range(4):
                oa = OA[:, ts * 2048 + hs * 512: ts * 2048 + (hs + 1) * 512]
                if sb == 0:
                    b.emit("vector", lambda v, oa=oa, hs=hs: v.tensor_copy(oa, psAs[hs][:, 0:512]),
                           reads=[("psA", hs)], writes=[("OA", ts, hs)])
                else:
                    b.emit("vector", lambda v, oa=oa, hs=hs: v.tensor_tensor(oa, oa, psAs[hs][:, 0:512], ALU.add),
                           reads=[("psA", hs), ("OA", ts, hs)], writes=[("OA", ts, hs)])
    X = t["X"]
    for ts in range(4):
        for hc in range(NCHUNK):
            src = OA[:, ts * 2048 + hc * 128: ts * 2048 + (hc + 1) * 128]
            b.emit("tensor", lambda tt, src=src: tt.transpose(psB0[:, 0:128], src, t["IDNF"][:]),
                   reads=[("OA", ts, hc // 4), "IDNF"], writes=["psB0"])
            pe_fence(K, ["psB0"])
            xs = X[:, hc * TL + t0 + ts * 128: hc * TL + t0 + (ts + 1) * 128]
            if layer == 0:
                b.emit("vector", lambda v, hc=hc: v.tensor_scalar(t["FF_T1"][:, 0:128], psB0[:, 0:128],
                                                                  t["DSH"][:, hc:hc + 1], None, ALU.mult),
                       reads=["psB0", "DSH"], writes=["FF_T1"])
            else:
                b.emit("vector", lambda v: v.tensor_scalar(t["FF_T1"][:, 0:128], psB0[:, 0:128],
                                                           0.5, None, ALU.mult),
                       reads=["psB0"], writes=["FF_T1"])
            b.emit("vector", lambda v, xs=xs: v.tensor_tensor(xs, xs, t["FF_T1"][:, 0:128], ALU.add),
                   reads=["FF_T1", ("X", hc)], writes=[("X", hc)])


def emit_load_x_fm(K, x_fm_dram):
    b = K.b
    X = K.t["X"]
    for k in range(NCHUNK):
        b.dma("sync", "xin", X[:, k * TL:(k + 1) * TL], x_fm_dram[k * 128:(k + 1) * 128, :],
              writes=[("X", k)])


def emit_store_x_fm(K, x_fm_dram):
    b = K.b
    X = K.t["X"]
    toks = []
    for k in range(NCHUNK):
        toks.append(("xout", k))
        b.dma("gpsimd", "xout", x_fm_dram[k * 128:(k + 1) * 128, :], X[:, k * TL:(k + 1) * TL],
              reads=[("X", k)], writes=[("xout", k)])
    return toks



def _host_idn():
    return np.eye(128, dtype=np.float32).astype(ml_dtypes.bfloat16)


def _host_idnf():
    return np.eye(128, dtype=np.float32)


def build_ffn_launch(layer):
    nc = bass.Bass("TRN2", target_bir_lowering=False)
    INTER = 5632 if layer == 0 else 8192
    x_in = nc.dram_tensor("x_fm", [D, TL], F32, kind="ExternalInput").ap()
    wg = nc.dram_tensor("wg", [D, INTER], F32, kind="ExternalInput").ap()
    wu = nc.dram_tensor("wu", [D, INTER], F32, kind="ExternalInput").ap()
    dw = nc.dram_tensor("dw", [INTER, D], F32, kind="ExternalInput").ap()
    if layer == 0:
        gs = nc.dram_tensor("gs", [1, INTER], F32, kind="ExternalInput").ap()
        us = nc.dram_tensor("us", [INTER // 128, 128], F32, kind="ExternalInput").ap()
        ds = nc.dram_tensor("ds", [D // 128, 128], F32, kind="ExternalInput").ap()
    idn = nc.dram_tensor("idn", [128, 128], BF16, kind="ExternalInput").ap()
    idnf = nc.dram_tensor("idnf", [128, 128], F32, kind="ExternalInput").ap()
    cpd = nc.dram_tensor("cpool", [128, 32], F32, kind="ExternalInput").ap()
    x_out = nc.dram_tensor("x_out", [D, TL], F32, kind="ExternalOutput").ap()

    es = contextlib.ExitStack()
    cp = ConstPool(nc)
    cp.alloc(es)
    b = Builder(nc)
    K = Kern(nc, b, cp, es)
    with es:
        alloc_common(K)
        alloc_ffn(K, INTER)
        if layer == 0:
            alloc_ffn0_scales(K)
        setup_idn(K, idn, idnf)
        load_const_pool(K, cpd)
        emit_load_x_fm(K, x_in)
        if layer == 0:
            emit_ffn0_scales(K, gs, us, ds)
        wcfg = {"wg": wg, "wu": wu, "dw": dw}
        for half in (0, 1):
            emit_ffn(K, layer, half, wcfg)
        emit_store_x_fm(K, x_out)
        b.finalize_stores(["xout"])
        b.build(es)
    return nc, cp




GATE_OFFSET = 6.906768
LOGD_HALF = 0.5 * float(np.log(128.0))


def alloc_attn_common(K):
    K.sb("ONESF", [128, 1], F32)
    K.b.emit("vector", lambda v: v.memset(K.t["ONESF"][:], 1.0), writes=["ONESF"])


def emit_proj_fm(K, name, w_dram, oc_n, wbuf, wfam):
    b, t = K.b, K.t
    XLN = t["XLN"]
    psBs = {"psB0": t["psB0"], "psB1": t["psB1"]}
    OUT = t[name]
    WB = t[wbuf]
    for oc in range(oc_n):
        par = oc % 2
        src = w_dram[:, oc * 128:(oc + 1) * 128].rearrange("(kc p) o -> p kc o", p=128)
        dst = WB[:, par * 2048:(par + 1) * 2048].rearrange("p (kc o) -> p kc o", o=128)
        b.dma("gpsimd", wfam + str(par), dst, src, writes=[(wbuf, par)])
        for half in range(2):
            pk = "psB0" if half == 0 else "psB1"
            PS = psBs[pk]
            for kc in range(NCHUNK):
                lhsT = WB[:, par * 2048 + kc * 128:(par * 2048) + (kc + 1) * 128]
                rhs = XLN[:, kc * 1024 + half * 512: kc * 1024 + half * 512 + 512]
                b.emit("tensor", lambda tt, lhsT=lhsT, rhs=rhs, PS=PS, kc=kc: tt.matmul(
                    PS[:, 0:512], lhsT, rhs, start=(kc == 0), stop=(kc == NCHUNK - 1)),
                    reads=[(wbuf, par), ("XLN", kc)], writes=[pk])
            dsts = OUT[:, oc * TL + half * 512: oc * TL + half * 512 + 512]
            b.emit("vector", lambda v, dsts=dsts, PS=PS: v.tensor_copy(dsts, PS[:, 0:512]),
                   reads=[pk], writes=[(name, oc)])


def emit_proj0(K, wq, wk, wv, wg):
    b, t = K.b, K.t
    emit_proj_fm(K, "QFL", wq, 16, "WPB", "wpq")
    emit_proj_fm(K, "KFL", wk, 8, "WPB", "wpk")
    emit_proj_fm(K, "VFL", wv, 8, "WPB", "wpv")
    XLN, psB0, psT = t["XLN"], t["psB0"], t["psT"]
    src = wg.rearrange("(kc p) o -> p kc o", p=128)
    b.dma("gpsimd", "wpg", t["WGQ"][:].rearrange("p (kc o) -> p kc o", o=8), src, writes=["WGQ"])
    LGTM, LGF = t["LGTM"], t["LGF"]
    cp = K.cp
    for sub in range(8):
        for kc in range(NCHUNK):
            lhsT = XLN[:, kc * 1024 + sub * 128: kc * 1024 + (sub + 1) * 128]
            b.emit("tensor", lambda tt, lhsT=lhsT, kc=kc: tt.matmul(
                psB0[:, 0:8], lhsT, t["WGQ"][:, kc * 8:(kc + 1) * 8],
                start=(kc == 0), stop=(kc == NCHUNK - 1)),
                reads=[("XLN", kc), "WGQ"], writes=["psB0"])
        b.emit("scalar", lambda s: s.activation(LGTM[:, 0:8], psB0[:, 0:8], AF.Exp,
                                                scale=-1.0, bias=cp.get(-GATE_OFFSET)),
               reads=["psB0", "CONST"], writes=["LGTM"])
        b.emit("scalar", lambda s: s.activation(LGTM[:, 8:16], LGTM[:, 0:8], AF.Ln, bias=cp.get(1.0)),
               reads=["LGTM", "CONST"], writes=["LGTM2"])
        b.emit("tensor", lambda tt: tt.transpose(psB0[0:8, 0:128], LGTM[:, 8:16], t["IDNF"][:]),
               reads=["LGTM2", "IDNF"], writes=["psB0"])
        pe_fence(K, ["psB0"])
        b.emit("vector", lambda v, sub=sub: v.tensor_copy(LGF[0:8, sub * 128:(sub + 1) * 128],
                                                          psB0[0:8, 0:128]),
               reads=["psB0"], writes=["LGF"])


def emit_proj0_out(K, qf_d, kf_d, vf_d, lg_d):
    b, t = K.b, K.t
    for oc in range(16):
        b.dma("gpsimd", "pout", qf_d[oc * 128:(oc + 1) * 128, :], t["QFL"][:, oc * TL:(oc + 1) * TL],
              reads=[("QFL", oc)], writes=[("po", "q", oc)])
    for oc in range(8):
        b.dma("gpsimd", "pout", kf_d[oc * 128:(oc + 1) * 128, :], t["KFL"][:, oc * TL:(oc + 1) * TL],
              reads=[("KFL", oc)], writes=[("po", "k", oc)])
        b.dma("gpsimd", "pout", vf_d[oc * 128:(oc + 1) * 128, :], t["VFL"][:, oc * TL:(oc + 1) * TL],
              reads=[("VFL", oc)], writes=[("po", "v", oc)])
    b.dma("gpsimd", "pout", lg_d, t["LGF"][0:8, :], reads=["LGF"], writes=[("po", "lg")])


def alloc_proj0(K):
    K.sb("XLN", [128, NCHUNK * 1024], BF16)
    K.sb("QFL", [128, 16 * TL], BF16)
    K.sb("KFL", [128, 8 * TL], BF16)
    K.sb("VFL", [128, 8 * TL], BF16)
    K.sb("WPB", [128, 2 * 2048], BF16)
    K.sb("WGQ", [128, 16 * 8], BF16)
    K.sb("LGTM", [128, 16], F32)
    K.sb("LGF", [8, TL], F32)


def build_proj0_launch():
    nc = bass.Bass("TRN2", target_bir_lowering=False)
    x_in = nc.dram_tensor("x_fm", [D, TL], F32, kind="ExternalInput").ap()
    wq = nc.dram_tensor("wq", [D, 2048], F32, kind="ExternalInput").ap()
    wk = nc.dram_tensor("wk", [D, 1024], F32, kind="ExternalInput").ap()
    wv = nc.dram_tensor("wv", [D, 1024], F32, kind="ExternalInput").ap()
    wg = nc.dram_tensor("wg", [D, 8], F32, kind="ExternalInput").ap()
    idn = nc.dram_tensor("idn", [128, 128], BF16, kind="ExternalInput").ap()
    idnf = nc.dram_tensor("idnf", [128, 128], F32, kind="ExternalInput").ap()
    cpd = nc.dram_tensor("cpool", [128, 32], F32, kind="ExternalInput").ap()
    qf = nc.dram_tensor("qf", [2048, TL], BF16, kind="ExternalOutput").ap()
    kf = nc.dram_tensor("kf", [1024, TL], BF16, kind="ExternalOutput").ap()
    vf = nc.dram_tensor("vf", [1024, TL], BF16, kind="ExternalOutput").ap()
    lg = nc.dram_tensor("lg", [8, TL], F32, kind="ExternalOutput").ap()

    es = contextlib.ExitStack()
    cp = ConstPool(nc)
    cp.alloc(es)
    b = Builder(nc)
    K = Kern(nc, b, cp, es)
    with es:
        alloc_common(K)
        alloc_proj0(K)
        setup_idn(K, idn, idnf)
        load_const_pool(K, cpd)
        emit_load_x_fm(K, x_in)
        emit_ln(K, 0, 512, out_col0=0, xln_w=1024)
        emit_ln(K, 512, 512, out_col0=512, xln_w=1024)
        emit_proj0(K, wq, wk, wv, wg)
        emit_proj0_out(K, qf, kf, vf, lg)
        b.finalize_stores(["pout"])
        b.build(es)
    return nc, cp



def alloc_attn0core(K):
    K.sb("KRES", [128, T], BF16)
    K.sb("VT", [128, 64 * 128], BF16)
    K.sb("VSTG", [128, 2 * 1024], BF16)
    K.sb("QTB", [128, 2 * 512], BF16)
    K.sb("QP", [128, 512], BF16)
    K.sb("AB", [128, 4 * 512], BF16)
    K.sb("YH", [128, 512], BF16)
    K.sb("RBS", [128, 512], F32)
    K.sb("RSB", [1, 512], F32)
    K.sb("LG0", [128, 64], F32)
    K.sb("LGC", [128, 128], F32)
    K.sb("LGPT", [128, 64], F32)
    K.sb("RFSQ", [128, 64], F32)
    K.sb("CSP", [128, 64], F32)
    K.sb("CT", [64, 128], F32)
    K.sb("OFS", [64, 1], F32)
    K.sb("CF4", [4, 128], F32)
    K.sb("TRIU", [128, 128], F32)
    K.sb("TRIS", [128, 128], F32)
    K.sb("TRIMASK", [128, 128], BF16)
    K.sb("SEL", [4, 4 * 128], F32)


def emit_lg_cumsum(K, lgh_view, off_dram):
    b, t = K.b, K.t
    psA0, psA1 = t["psA0"], t["psA1"]
    psB0 = t["psB0"]
    b.emit("vector", lambda v: v.memset(t["LGC"][:], 0.0), writes=["LGC"])
    b.emit("vector", lambda v: v.memset(t["CSP"][:], 0.0), writes=["CSP"])
    b.dma("sync", "lgh", t["LGC"][0:64, 0:128], lgh_view, writes=["LGC"])
    b.emit("tensor", lambda tt: tt.transpose(psB0[:, 0:128], t["LGC"][:, 0:128], t["IDNF"][:]),
           reads=["LGC", "IDNF"], writes=["psB0"])
    pe_fence(K, ["psB0"])
    b.emit("vector", lambda v: v.tensor_copy(t["LG0"][:], psB0[:, 0:64]),
           reads=["psB0"], writes=["LG0"])
    b.emit("tensor", lambda tt: tt.matmul(psA0[:, 0:64], t["TRIU"][:], t["LG0"][:],
                                          start=True, stop=True),
           reads=["TRIU", "LG0"], writes=[("psA", 0)])
    b.emit("vector", lambda v: v.tensor_copy(t["LGC"][:, 0:64], psA0[:, 0:64]),
           reads=[("psA", 0)], writes=["LGC"])
    b.emit("tensor", lambda tt: tt.matmul(psA1[0:1, 0:64], t["ONESF"][:, 0:1], t["LG0"][:],
                                          start=True, stop=True),
           reads=["ONESF", "LG0"], writes=[("psA", 1)])
    b.emit("vector", lambda v: v.tensor_copy(t["CSP"][0:1, :], psA1[0:1, 0:64]),
           reads=[("psA", 1)], writes=["CSP"])
    b.emit("tensor", lambda tt: tt.transpose(psB0[0:64, 0:128], t["CSP"][:, 0:64], t["IDNF"][:]),
           reads=["CSP", "IDNF"], writes=["psB0"])
    pe_fence(K, ["psB0"])
    b.emit("vector", lambda v: v.tensor_copy(t["CT"][0:64, 0:1], psB0[0:64, 0:1]),
           reads=["psB0"], writes=["CT"])
    b.emit("tensor", lambda tt: tt.matmul(psA1[0:64, 0:1], t["TRIS"][0:64, 0:64], t["CT"][0:64, 0:1],
                                          start=True, stop=True),
           reads=["TRIS", "CT"], writes=[("psA", 1)])
    b.emit("vector", lambda v: v.tensor_copy(t["OFS"][:], psA1[0:64, 0:1]),
           reads=[("psA", 1)], writes=["OFS"])
    b.dma("gpsimd", "ofs", off_dram, t["OFS"][:].rearrange("a () -> a"), reads=["OFS"], writes=["OFSD"])
    b.dma("gpsimd", "ofs", t["CSP"][0:1, :], off_dram.rearrange("a -> () a"), reads=["OFSD"], writes=["CSPR"])
    b.emit("tensor", lambda tt: tt.matmul(psB0[:, 0:64], t["ONESROW"][0:1, :], t["CSP"][0:1, 0:64],
                                          start=True, stop=True),
           reads=["CSPR", "ONESROW"], writes=["psB0"])
    b.emit("vector", lambda v: v.tensor_tensor(t["LGPT"][:], t["LGC"][:, 0:64], psB0[:, 0:64], ALU.add),
           reads=["LGC", "psB0"], writes=["LGPT"])
    b.emit("scalar", lambda s: s.activation(t["RFSQ"][:], t["LGPT"][:], AF.Exp,
                                            scale=0.5, bias=K.cp.get(-LOGD_HALF)),
           reads=["LGPT", "CONST"], writes=["RFSQ"])


def emit_v_transpose(K, vhf_fn, vt_name, nd, aug, nrows=128):
    b, t = K.b, K.t
    psT = t["psT"]
    VT, VSTG = t[vt_name], t["VSTG"]
    w = nd + (1 if aug else 0)
    if aug:
        b.emit("vector", lambda v: v.memset(VT[:], 1.0), writes=[(vt_name, "init")])
    for gblk in range(8):
        par = gblk % 2
        b.dma("sync", "vstg%d" % par, VSTG[0:nrows, par * 1024:(par * 1024) + 1024],
              vhf_fn(gblk), writes=[("VSTG", par)])
        for j in range(8):
            src = VSTG[:, par * 1024 + j * 128: par * 1024 + (j + 1) * 128]
            b.emit("tensor", lambda tt, src=src, j=j: tt.transpose(
                psT[:, j * 128:(j + 1) * 128], src, t["IDN"][:]),
                reads=[("VSTG", par), "IDN"], writes=["psT"])
        pe_fence(K, ["psT"])
        g0 = gblk * 8
        if aug:
            dst = VT[:, g0 * w: (g0 + 8) * w].rearrange("p (a q) -> p a q", q=w)[:, :, 1:1 + nd]
        else:
            dst = VT[:, g0 * w: (g0 + 8) * w].rearrange("p (a q) -> p a q", q=w)
        srcv = psT[:, 0:1024].rearrange("p (a q) -> p a q", q=128)[:, :, 0:nd]
        b.emit("vector", lambda v, dst=dst, srcv=srcv: v.tensor_copy(dst, srcv),
               reads=["psT", (vt_name, "init")] if aug else ["psT"], writes=[(vt_name, gblk)])


def emit_attn0_core(K, q_fn, y_fn):
    b, t = K.b, K.t
    psA0, psA1, psA2, psA3 = t["psA0"], t["psA1"], t["psA2"], t["psA3"]
    psB0, psB1, psC = t["psB0"], t["psB1"], t["psC"]
    QTB, QP, AB, YH, RBS, RSB = t["QTB"], t["QP"], t["AB"], t["YH"], t["RBS"], t["RSB"]
    for qt in range(16):
        b.emit("tensor", lambda tt, qt=qt: tt.transpose(psB1[0:4, 0:128],
                                                        t["LGPT"][:, 4 * qt:4 * qt + 4], t["IDNF"][:]),
               reads=["LGPT", "IDNF"], writes=["psB1"])
        pe_fence(K, ["psB1"])
        b.emit("scalar", lambda s: s.activation(t["CF4"][:], psB1[0:4, 0:128], AF.Exp, scale=-0.5),
               reads=["psB1"], writes=["CF4"])
        for j in range(4):
            b.emit("tensor", lambda tt, j=j: tt.matmul(psB0[:, j * 128:(j + 1) * 128],
                                                       t["SEL"][0:4, j * 128:(j + 1) * 128], t["CF4"][:],
                                                       start=True, stop=True, skip_group_check=True),
                   reads=["SEL", "CF4"], writes=["psB0"])
        nsc = 4 * qt + 4
        for h in range(2):
            qpar = (qt * 2 + h) % 2
            b.dma("sync", "qtl%d" % qpar, QTB[:, qpar * 512:(qpar + 1) * 512],
                  q_fn(h, qt), writes=[("QTB", qpar)])
            b.emit("vector", lambda v, qpar=qpar: v.tensor_tensor(
                QP[:], QTB[:, qpar * 512:(qpar + 1) * 512], psB0[:, 0:512], ALU.mult),
                reads=[("QTB", qpar), "psB0"], writes=["QP"])
            YPS = psA0 if h == 0 else psA1
            NPS = psA2 if h == 0 else psA3
            for sc in range(nsc):
                spar = sc % 2
                apar = sc % 4
                SP = psB1[:, 0:512] if spar == 0 else psC[:, 0:512]
                spk = "psB1" if spar == 0 else "psC"
                b.emit("tensor", lambda tt, SP=SP, sc=sc: tt.matmul(
                    SP, t["KRES"][:, sc * 128:(sc + 1) * 128], QP[:], start=True, stop=True),
                    reads=["KRES", "QP"], writes=[spk])
                abs_ = AB[:, apar * 512:(apar + 1) * 512]
                b.emit("scalar", lambda s, abs_=abs_, SP=SP, sc=sc: s.activation(
                    abs_, SP, AF.Square, scale=t["RFSQ"][:, sc:sc + 1]),
                    reads=[spk, "RFSQ"], writes=[("AB", apar)])
                if sc >= 4 * qt:
                    rel = sc - 4 * qt
                    if rel > 0:
                        b.emit("vector", lambda v, abs_=abs_, rel=rel: v.memset(abs_[:, 0:rel * 128], 0.0),
                               reads=[("AB", apar)], writes=[("AB", apar)])
                    b.emit("vector", lambda v, abs_=abs_, rel=rel: v.tensor_tensor(
                        abs_[:, rel * 128:(rel + 1) * 128], abs_[:, rel * 128:(rel + 1) * 128],
                        t["TRIMASK"][:], ALU.mult),
                        reads=[("AB", apar), "TRIMASK"], writes=[("AB", apar)])
                b.emit("tensor", lambda tt, abs_=abs_, sc=sc, YPS=YPS: tt.matmul(
                    YPS[:, 0:512], t["VT"][:, sc * 128:(sc + 1) * 128], abs_,
                    start=(sc == 0), stop=(sc == nsc - 1), skip_group_check=True),
                    reads=[("AB", apar), ("VT", sc // 8)], writes=[("psA", h)])
                b.emit("tensor", lambda tt, abs_=abs_, sc=sc, NPS=NPS: tt.matmul(
                    NPS[0:1, 0:512], t["ONESB"][:, 0:1], abs_,
                    start=(sc == 0), stop=(sc == nsc - 1), skip_group_check=True),
                    reads=[("AB", apar), "ONESB"], writes=[("psA", 2 + h)])
            b.emit("vector", lambda v, NPS=NPS: v.tensor_scalar(
                RSB[:], NPS[0:1, 0:512], 1e-6, None, ALU.add),
                reads=[("psA", 2 + h)], writes=["RSB"])
            b.emit("vector", lambda v: v.reciprocal(RSB[:], RSB[:]), reads=["RSB"], writes=["RSB"])
            b.emit("tensor", lambda tt: tt.matmul(psC[:, 0:512], t["ONESROW"][0:1, :], RSB[:],
                                                  start=True, stop=True),
                   reads=["RSB", "ONESROW"], writes=["psC"])
            b.emit("scalar", lambda s: s.copy(RBS[:], psC[:, 0:512]), reads=["psC"], writes=["RBS"])
            b.emit("vector", lambda v, YPS=YPS: v.tensor_tensor(
                YH[:], YPS[:, 0:512], RBS[:], ALU.mult),
                reads=[("psA", h), "RBS"], writes=["YH"])
            b.dma("gpsimd", "yst", y_fn(h, qt), YH[:],
                  reads=["YH"], writes=[("yst", qt, h)])


def emit_load_kres(K, kh_fn, nrows=128):
    b = K.b
    for j in range(8):
        b.dma("sync", "kres", K.t["KRES"][0:nrows, j * 1024:(j + 1) * 1024],
              kh_fn(j), writes=[("KRES", j)])
    K.b.res["KRES"] = [K.b.res[("KRES", 7)][0], []]


def _host_triu():
    return np.triu(np.ones((128, 128), np.float32))


def _host_tris():
    return np.triu(np.ones((128, 128), np.float32), 1)


def _host_trimask():
    return np.triu(np.ones((128, 128), np.float32)).astype(ml_dtypes.bfloat16)


def _host_sel():
    s = np.zeros((4, 512), np.float32)
    for j in range(4):
        s[j, j * 128:(j + 1) * 128] = 1.0
    return s


def setup_attn0_consts(K, triu_d, tris_d, trimask_d, sel_d):
    b = K.b
    b.dma("sync", "f_tri", K.t["TRIU"][:], triu_d, writes=["TRIU"])
    b.dma("sync", "f_tri", K.t["TRIS"][:], tris_d, writes=["TRIS"])
    b.dma("sync", "f_tri", K.t["TRIMASK"][:], trimask_d, writes=["TRIMASK"])
    b.dma("sync", "f_tri", K.t["SEL"][:], sel_d, writes=["SEL"])


def build_attn0core_launch():
    nc = bass.Bass("TRN2", target_bir_lowering=False)
    kh = nc.dram_tensor("kh", [128, T], BF16, kind="ExternalInput").ap()
    vhf = nc.dram_tensor("vhf", [128, T], BF16, kind="ExternalInput").ap()
    qh = nc.dram_tensor("qh", [256, T], BF16, kind="ExternalInput").ap()
    lgh = nc.dram_tensor("lgh", [64, 128], F32, kind="ExternalInput").ap()
    idn = nc.dram_tensor("idn", [128, 128], BF16, kind="ExternalInput").ap()
    idnf = nc.dram_tensor("idnf", [128, 128], F32, kind="ExternalInput").ap()
    cpd = nc.dram_tensor("cpool", [128, 32], F32, kind="ExternalInput").ap()
    triu_d = nc.dram_tensor("triu", [128, 128], F32, kind="ExternalInput").ap()
    tris_d = nc.dram_tensor("tris", [128, 128], F32, kind="ExternalInput").ap()
    trimask_d = nc.dram_tensor("trimask", [128, 128], BF16, kind="ExternalInput").ap()
    sel_d = nc.dram_tensor("sel", [4, 512], F32, kind="ExternalInput").ap()
    off_d = nc.dram_tensor("offsc", [64], F32, kind="Internal").ap()
    yh = nc.dram_tensor("yh", [256, T], BF16, kind="ExternalOutput").ap()

    es = contextlib.ExitStack()
    cp = ConstPool(nc)
    cp.alloc(es)
    b = Builder(nc)
    K = Kern(nc, b, cp, es)
    with es:
        for i in range(4):
            K.ps("psA%d" % i, [128, 512], F32)
        K.ps("psB0", [128, 512], F32)
        K.ps("psB1", [128, 512], F32)
        K.ps("psC", [128, 512], F32)
        K.ps("psT", [128, 1024], BF16)
        K.sb("ONESB", [128, 1], BF16)
        K.sb("ONESROW", [1, 128], F32)
        K.sb("IDN", [128, 128], BF16)
        K.sb("IDNF", [128, 128], F32)
        b.emit("vector", lambda v: v.memset(K.t["ONESB"][:], 1.0), writes=["ONESB"])
        b.emit("vector", lambda v: v.memset(K.t["ONESROW"][:], 1.0), writes=["ONESROW"])
        alloc_attn_common(K)
        alloc_attn0core(K)
        setup_idn(K, idn, idnf)
        load_const_pool(K, cpd)
        setup_attn0_consts(K, triu_d, tris_d, trimask_d, sel_d)
        emit_load_kres(K, lambda j: kh[:, j * 1024:(j + 1) * 1024])
        emit_v_transpose(K, lambda g: vhf[:, g * 1024:(g + 1) * 1024], "VT", 128, aug=False)
        emit_lg_cumsum(K, lgh, off_d)
        emit_attn0_core(K, lambda h, qt: qh[h * 128:(h + 1) * 128, qt * 512:(qt + 1) * 512],
                        lambda h, qt: yh[h * 128:(h + 1) * 128, qt * 512:(qt + 1) * 512])
        b.finalize_stores(["yst"])
        b.build(es)
    return nc, cp



def alloc_outproj(K):
    K.sb("YT", [128, NCHUNK * TL], BF16)
    K.sb("WCB", [128, 4 * 2048], BF16)
    K.sb("OC", [128, NCHUNK * 512], F32)
    K.sb("BCH", [128, 16], F32)
    K.sb("OP_T1", [128, 512], F32)


def emit_load_yt(K, yt_src):
    b = K.b
    for mc in range(NCHUNK):
        b.dma("sync", "ytl", K.t["YT"][:, mc * TL:(mc + 1) * TL], yt_src(mc),
              writes=[("YT", mc)])


def emit_bias_load(K, bias_dram):
    b, t = K.b, K.t
    ch = t["OC"][0:128, 0:128]
    b.emit("vector", lambda v: v.memset(ch, 0.0), writes=["OCB_STG"])
    b.dma("sync", "f_bc", ch[0:16, :], bias_dram, reads=["OCB_STG"], writes=["OCB_STG"])
    b.emit("tensor", lambda tt: tt.transpose(t["psB0"][:, 0:128], ch, t["IDNF"][:]),
           reads=["OCB_STG", "IDNF"], writes=["psB0"])
    pe_fence(K, ["psB0"])
    b.emit("vector", lambda v: v.tensor_copy(t["BCH"][:], t["psB0"][:, 0:16]),
           reads=["psB0"], writes=["BCH"])
    K.b.barrier()


def emit_out_proj(K, w_dram):
    b, t = K.b, K.t
    YT, WCB, OC = t["YT"], t["WCB"], t["OC"]
    psAs = [t["psA%d" % i] for i in range(4)]
    X = t["X"]
    for half in range(2):
        tc0 = half * 512
        for sbm in range(4):
            for j in range(4):
                mc = sbm * 4 + j
                src = w_dram[mc * 128:(mc + 1) * 128, :]
                b.dma("gpsimd", "wcb%d" % (mc % 2), WCB[:, j * 2048:(j + 1) * 2048], src,
                      writes=[("WCB", j)])
            for cc in range(NCHUNK):
                pa = cc % 4
                for j in range(4):
                    mc = sbm * 4 + j
                    lhsT = WCB[:, j * 2048 + cc * 128: j * 2048 + (cc + 1) * 128]
                    rhs = YT[:, mc * TL + tc0: mc * TL + tc0 + 512]
                    b.emit("tensor", lambda tt, lhsT=lhsT, rhs=rhs, pa=pa, j=j: tt.matmul(
                        psAs[pa][:, 0:512], lhsT, rhs, start=(j == 0), stop=(j == 3),
                        skip_group_check=True),
                        reads=[("WCB", j), ("YT", mc)], writes=[("psA", pa)])
                oc = OC[:, cc * 512:(cc + 1) * 512]
                if sbm == 0:
                    b.emit("vector", lambda v, oc=oc, pa=pa: v.tensor_copy(oc, psAs[pa][:, 0:512]),
                           reads=[("psA", pa)], writes=[("OC", cc)])
                else:
                    b.emit("vector", lambda v, oc=oc, pa=pa: v.tensor_tensor(oc, oc, psAs[pa][:, 0:512], ALU.add),
                           reads=[("psA", pa), ("OC", cc)], writes=[("OC", cc)])
        for cc in range(NCHUNK):
            oc = OC[:, cc * 512:(cc + 1) * 512]
            xs = X[:, cc * TL + tc0: cc * TL + tc0 + 512]
            b.emit("vector", lambda v, oc=oc, cc=cc: v.tensor_scalar(
                t["OP_T1"][:], oc, t["BCH"][:, cc:cc + 1], None, ALU.add),
                reads=[("OC", cc), "BCH"], writes=["OP_T1"])
            b.emit("vector", lambda v, xs=xs: v.tensor_tensor(xs, xs, t["OP_T1"][:], ALU.add),
                   reads=["OP_T1", ("X", cc)], writes=[("X", cc)])


def build_outproj_launch():
    nc = bass.Bass("TRN2", target_bir_lowering=False)
    x_in = nc.dram_tensor("x_fm", [D, TL], F32, kind="ExternalInput").ap()
    yt_in = nc.dram_tensor("yt", [D, TL], BF16, kind="ExternalInput").ap()
    w = nc.dram_tensor("w", [D, D], F32, kind="ExternalInput").ap()
    bias = nc.dram_tensor("bias", [16, 128], F32, kind="ExternalInput").ap()
    idn = nc.dram_tensor("idn", [128, 128], BF16, kind="ExternalInput").ap()
    idnf = nc.dram_tensor("idnf", [128, 128], F32, kind="ExternalInput").ap()
    cpd = nc.dram_tensor("cpool", [128, 32], F32, kind="ExternalInput").ap()
    x_out = nc.dram_tensor("x_out", [D, TL], F32, kind="ExternalOutput").ap()

    es = contextlib.ExitStack()
    cp = ConstPool(nc)
    cp.alloc(es)
    b = Builder(nc)
    K = Kern(nc, b, cp, es)
    with es:
        alloc_common(K)
        alloc_outproj(K)
        setup_idn(K, idn, idnf)
        load_const_pool(K, cpd)
        emit_load_x_fm(K, x_in)
        emit_load_yt(K, lambda mc: yt_in[mc * 128:(mc + 1) * 128, :])
        emit_bias_load(K, bias)
        emit_out_proj(K, w)
        emit_store_x_fm(K, x_out)
        b.finalize_stores(["xout"])
        b.build(es)
    return nc, cp



def alloc_proj1(K):
    K.sb("XLN", [128, NCHUNK * 1024], BF16)
    K.sb("QFL", [128, 16 * TL], BF16)
    K.sb("KFL", [128, 4 * TL], BF16)
    K.sb("VFL", [128, 4 * TL], BF16)
    K.sb("WPB", [128, 2 * 2048], BF16)
    K.sb("WTKB", [128, 8192], BF16)
    K.sb("QTM", [128, 512], F32)
    K.sb("ROT", [128, 512], F32)
    K.sb("RP1", [128, 512], F32)
    K.sb("RP2", [128, 512], F32)
    K.sb("QR", [128, 512], BF16)
    K.sb("COSR", [128, 2 * 512], F32)
    K.sb("SINR", [128, 2 * 512], F32)


def emit_proj1(K, wq, wk, wv, cos_d, sin_d):
    b, t = K.b, K.t
    XLN, psT = t["XLN"], t["psT"]
    psAs = [t["psA%d" % i] for i in range(4)]
    emit_proj_fm(K, "VFL", wv, 4, "WPB", "wpv")
    for ocb in range(4):
        src = wq[:, ocb * 512:(ocb + 1) * 512].rearrange("(kc p) o -> p kc o", p=128)
        dst = t["WTKB"][:].rearrange("p (kc o) -> p kc o", o=512)
        b.dma("gpsimd", "wq1", dst, src, writes=["WTKB"])
        for sub in range(8):
            for kc in range(NCHUNK):
                lhsT = XLN[:, kc * 1024 + sub * 128: kc * 1024 + (sub + 1) * 128]
                rhs = t["WTKB"][:, kc * 512:(kc + 1) * 512]
                b.emit("tensor", lambda tt, lhsT=lhsT, rhs=rhs, ocb=ocb, kc=kc: tt.matmul(
                    psAs[ocb % 4][:, 0:512], lhsT, rhs, start=(kc == 0), stop=(kc == NCHUNK - 1),
                    skip_group_check=True),
                    reads=["WTKB", ("XLN", kc)], writes=[("psA", ocb % 4)])
            cpar = sub % 2
            b.dma("sync", "cos%d" % cpar, t["COSR"][:, cpar * 512:(cpar + 1) * 512],
                  cos_d[sub * 128:(sub + 1) * 128, ocb * 512:(ocb + 1) * 512], writes=[("COSR", cpar)])
            b.dma("sync", "sin%d" % cpar, t["SINR"][:, cpar * 512:(cpar + 1) * 512],
                  sin_d[sub * 128:(sub + 1) * 128, ocb * 512:(ocb + 1) * 512], writes=[("SINR", cpar)])
            emit_rope_block(K, ocb, sub, cpar)
    src = wk.rearrange("(kc p) o -> p kc o", p=128)
    dst = t["WTKB"][:].rearrange("p (kc o) -> p kc o", o=512)
    b.dma("gpsimd", "wq1", dst, src, writes=["WTKB"])
    for sub in range(8):
        for kc in range(NCHUNK):
            lhsT = XLN[:, kc * 1024 + sub * 128: kc * 1024 + (sub + 1) * 128]
            rhs = t["WTKB"][:, kc * 512:(kc + 1) * 512]
            b.emit("tensor", lambda tt, lhsT=lhsT, rhs=rhs, kc=kc: tt.matmul(
                psAs[0][:, 0:512], lhsT, rhs, start=(kc == 0), stop=(kc == NCHUNK - 1),
                skip_group_check=True),
                reads=["WTKB", ("XLN", kc)], writes=[("psA", 0)])
        cpar = sub % 2
        b.dma("sync", "cos%d" % cpar, t["COSR"][:, cpar * 512:(cpar + 1) * 512],
              cos_d[sub * 128:(sub + 1) * 128, 0:512], writes=[("COSR", cpar)])
        b.dma("sync", "sin%d" % cpar, t["SINR"][:, cpar * 512:(cpar + 1) * 512],
              sin_d[sub * 128:(sub + 1) * 128, 0:512], writes=[("SINR", cpar)])
        emit_rope_k_block(K, sub, cpar)


def emit_rope_block(K, ocb, sub, cpar):
    b, t = K.b, K.t
    psT = t["psT"]
    QTM, ROT, RP1, RP2, QR = t["QTM"], t["ROT"], t["RP1"], t["RP2"], t["QR"]
    pa = ocb % 4
    co = cpar * 512
    b.emit("vector", lambda v: v.tensor_copy(QTM[:, 0:512], t["psA%d" % pa][:, 0:512]),
           reads=[("psA", pa)], writes=["QTM"])
    ev = QTM[:, 0:512].rearrange("p (g two) -> p g two", two=2)
    rv = ROT[:, 0:512].rearrange("p (g two) -> p g two", two=2)
    b.emit("vector", lambda v: v.tensor_scalar(rv[:, :, 0], ev[:, :, 1], -1.0, None, ALU.mult),
           reads=["QTM"], writes=["ROT"])
    b.emit("vector", lambda v: v.tensor_copy(rv[:, :, 1], ev[:, :, 0]),
           reads=["QTM"], writes=["ROT"])
    b.emit("vector", lambda v: v.tensor_tensor(RP1[:, 0:512], QTM[:, 0:512],
                                               t["COSR"][:, co:co + 512], ALU.mult),
           reads=["QTM", ("COSR", cpar)], writes=["RP1"])
    b.emit("vector", lambda v: v.tensor_tensor(RP2[:, 0:512], ROT[:, 0:512],
                                               t["SINR"][:, co:co + 512], ALU.mult),
           reads=["ROT", ("SINR", cpar)], writes=["RP2"])
    b.emit("vector", lambda v: v.tensor_tensor(QR[:, 0:512], RP1[:, 0:512], RP2[:, 0:512], ALU.add),
           reads=["RP1", "RP2"], writes=["QR"])
    for j in range(4):
        src = QR[:, j * 128:(j + 1) * 128]
        b.emit("tensor", lambda tt, src=src, j=j: tt.transpose(
            psT[:, j * 128:(j + 1) * 128], src, t["IDN"][:]),
            reads=["QR", "IDN"], writes=["psT"])
    pe_fence(K, ["psT"])
    QFL = t["QFL"]
    for j in range(4):
        oc = 4 * ocb + j
        dst = QFL[:, oc * TL + sub * 128: oc * TL + sub * 128 + 128]
        b.emit("vector", lambda v, dst=dst, j=j: v.tensor_copy(dst, psT[:, j * 128:(j + 1) * 128]),
               reads=["psT"], writes=[("QFL", oc)])


def emit_rope_k_block(K, sub, cpar):
    b, t = K.b, K.t
    psT = t["psT"]
    QTM, ROT, RP1, RP2, QR = t["QTM"], t["ROT"], t["RP1"], t["RP2"], t["QR"]
    co = cpar * 512
    b.emit("vector", lambda v: v.tensor_copy(QTM[:, 0:512], t["psA0"][:, 0:512]),
           reads=[("psA", 0)], writes=["QTM"])
    ev = QTM[:, 0:512].rearrange("p (g two) -> p g two", two=2)
    rv = ROT[:, 0:512].rearrange("p (g two) -> p g two", two=2)
    b.emit("vector", lambda v: v.tensor_scalar(rv[:, :, 0], ev[:, :, 1], -1.0, None, ALU.mult),
           reads=["QTM"], writes=["ROT"])
    b.emit("vector", lambda v: v.tensor_copy(rv[:, :, 1], ev[:, :, 0]),
           reads=["QTM"], writes=["ROT"])
    b.emit("vector", lambda v: v.tensor_tensor(RP1[:, 0:512], QTM[:, 0:512],
                                               t["COSR"][:, co:co + 512], ALU.mult),
           reads=["QTM", ("COSR", cpar)], writes=["RP1"])
    b.emit("vector", lambda v: v.tensor_tensor(RP2[:, 0:512], ROT[:, 0:512],
                                               t["SINR"][:, co:co + 512], ALU.mult),
           reads=["ROT", ("SINR", cpar)], writes=["RP2"])
    b.emit("vector", lambda v: v.tensor_tensor(QR[:, 0:512], RP1[:, 0:512], RP2[:, 0:512], ALU.add),
           reads=["RP1", "RP2"], writes=["QR"])
    for j in range(4):
        src = QR[:, j * 128:(j + 1) * 128]
        b.emit("tensor", lambda tt, src=src, j=j: tt.transpose(
            psT[:, j * 128:(j + 1) * 128], src, t["IDN"][:]),
            reads=["QR", "IDN"], writes=["psT"])
    pe_fence(K, ["psT"])
    KFL = t["KFL"]
    for j in range(4):
        dst = KFL[:, j * TL + sub * 128: j * TL + sub * 128 + 128]
        b.emit("vector", lambda v, dst=dst, j=j: v.tensor_copy(dst, psT[:, j * 128:(j + 1) * 128]),
               reads=["psT"], writes=[("KFL", j)])


def emit_proj1_out(K, qf_d, kf_d, vf_d):
    b, t = K.b, K.t
    for oc in range(16):
        b.dma("gpsimd", "pout", qf_d[oc * 128:(oc + 1) * 128, :], t["QFL"][:, oc * TL:(oc + 1) * TL],
              reads=[("QFL", oc)], writes=[("po", "q", oc)])
    for oc in range(4):
        b.dma("gpsimd", "pout", kf_d[oc * 128:(oc + 1) * 128, :], t["KFL"][:, oc * TL:(oc + 1) * TL],
              reads=[("KFL", oc)], writes=[("po", "k", oc)])
        b.dma("gpsimd", "pout", vf_d[oc * 128:(oc + 1) * 128, :], t["VFL"][:, oc * TL:(oc + 1) * TL],
              reads=[("VFL", oc)], writes=[("po", "v", oc)])


def build_proj1_launch():
    nc = bass.Bass("TRN2", target_bir_lowering=False)
    x_in = nc.dram_tensor("x_fm", [D, TL], F32, kind="ExternalInput").ap()
    wq = nc.dram_tensor("wq", [D, 2048], F32, kind="ExternalInput").ap()
    wk = nc.dram_tensor("wk", [D, 512], F32, kind="ExternalInput").ap()
    wv = nc.dram_tensor("wv", [D, 512], F32, kind="ExternalInput").ap()
    cos_d = nc.dram_tensor("cosr", [TL, 2048], F32, kind="ExternalInput").ap()
    sin_d = nc.dram_tensor("sinr", [TL, 2048], F32, kind="ExternalInput").ap()
    idn = nc.dram_tensor("idn", [128, 128], BF16, kind="ExternalInput").ap()
    idnf = nc.dram_tensor("idnf", [128, 128], F32, kind="ExternalInput").ap()
    cpd = nc.dram_tensor("cpool", [128, 32], F32, kind="ExternalInput").ap()
    qf = nc.dram_tensor("qf", [2048, TL], BF16, kind="ExternalOutput").ap()
    kf = nc.dram_tensor("kf", [512, TL], BF16, kind="ExternalOutput").ap()
    vf = nc.dram_tensor("vf", [512, TL], BF16, kind="ExternalOutput").ap()

    es = contextlib.ExitStack()
    cp = ConstPool(nc)
    cp.alloc(es)
    b = Builder(nc)
    K = Kern(nc, b, cp, es)
    with es:
        for i in range(4):
            K.ps("psA%d" % i, [128, 512], F32)
        K.ps("psB0", [128, 512], F32)
        K.ps("psB1", [128, 512], F32)
        K.ps("psC", [128, 512], F32)
        K.ps("psT", [128, 1024], BF16)
        K.sb("X", [128, NCHUNK * TL], F32)
        K.sb("ONESB", [128, 1], BF16)
        K.sb("ONESROW", [1, 128], F32)
        K.sb("IDN", [128, 128], BF16)
        K.sb("IDNF", [128, 128], F32)
        K.sb("LN_XBC", [128, 512], BF16)
        K.sb("LN_XSQ", [128, 512], BF16)
        K.sb("LN_MS", [1, 2048], F32)
        K.sb("LN_TMP", [128, 512], F32)
        b.emit("vector", lambda v: v.memset(K.t["ONESB"][:], 1.0), writes=["ONESB"])
        b.emit("vector", lambda v: v.memset(K.t["ONESROW"][:], 1.0), writes=["ONESROW"])
        alloc_proj1(K)
        setup_idn(K, idn, idnf)
        load_const_pool(K, cpd)
        emit_load_x_fm(K, x_in)
        emit_ln(K, 0, 512, out_col0=0, xln_w=1024)
        emit_ln(K, 512, 512, out_col0=512, xln_w=1024)
        emit_proj1(K, wq, wk, wv, cos_d, sin_d)
        emit_proj1_out(K, qf, kf, vf)
        b.finalize_stores(["pout"])
        b.build(es)
    return nc, cp


def alloc_attn1core(K):
    K.sb("KRES", [64, T], BF16)
    K.sb("VT", [128, 64 * 65], BF16)
    K.sb("VSTG", [128, 2 * 1024], BF16)
    K.sb("QTB", [64, 2 * 512], BF16)
    K.sb("AB", [128, 4 * 512], BF16)
    K.sb("ZB", [128, 2 * 512], F32)
    K.sb("TB", [128, 2 * 512], F32)
    K.sb("YH", [65, 512], BF16)
    K.sb("RBS", [65, 512], F32)
    K.sb("RSB", [1, 512], F32)
    K.sb("ALR", [1, 512], F32)
    K.sb("ALBS", [128, 512], F32)
    K.sb("SBV", [128, 4 * 64], F32)
    K.sb("ALRS", [1, 2 * 512], F32)
    K.sb("TRIMASK", [128, 128], BF16)


def emit_attn1_core_pre(K, q_fn, y_fn, alr_dram):
    b, t = K.b, K.t
    psA = [t["psA%d" % i] for i in range(4)]
    psB0, psB1, psC = t["psB0"], t["psB1"], t["psC"]
    QTB, AB, ZB, TB, YH = t["QTB"], t["AB"], t["ZB"], t["TB"], t["YH"]
    RSB, RBS, ALBS = t["RSB"], t["RBS"], t["ALBS"]
    cp = K.cp
    for qt in range(16):
        nsc = 4 * qt + 4
        for h in range(4):
            qpar = (qt * 4 + h) % 2
            b.dma("sync", "qtl%d" % qpar, QTB[:, qpar * 512:(qpar + 1) * 512],
                  q_fn(h, qt), writes=[("QTB", qpar)])
            b.dma("sync", "alr%d" % qpar, t["ALRS"][0:1, qpar * 512:(qpar + 1) * 512],
                  alr_dram[0:1, h * T + qt * 512: h * T + qt * 512 + 512],
                  writes=[("ALRS", qpar)])
            b.emit("tensor", lambda tt, qpar=qpar: tt.matmul(
                psB0[:, 0:512], t["ONESROW"][0:1, :],
                t["ALRS"][0:1, qpar * 512:(qpar + 1) * 512],
                start=True, stop=True),
                reads=[("ALRS", qpar), "ONESROW"], writes=["psB0"])
            b.emit("scalar", lambda s: s.copy(ALBS[:], psB0[:, 0:512]), reads=["psB0"], writes=["ALBS"])
            YPS = psA[h]
            for sc in range(nsc):
                spar = sc % 2
                apar = sc % 4
                SP = psB1[:, 0:512] if spar == 0 else psC[:, 0:512]
                spk = "psB1" if spar == 0 else "psC"
                ZBv = ZB[:, spar * 512:(spar + 1) * 512]
                TBv = TB[:, spar * 512:(spar + 1) * 512]
                b.emit("tensor", lambda tt, SP=SP, sc=sc, qpar=qpar: tt.matmul(
                    SP, t["KRES"][:, sc * 128:(sc + 1) * 128], QTB[:, qpar * 512:(qpar + 1) * 512],
                    start=True, stop=True),
                    reads=["KRES", ("QTB", qpar)], writes=[spk])
                b.emit("vector", lambda v, SP=SP, ZBv=ZBv: v.tensor_tensor(ZBv, SP, ALBS[:], ALU.add),
                       reads=[spk, "ALBS"], writes=[("ZB", spar)])
                b.emit("scalar", lambda s, sc=sc, h=h, ZBv=ZBv, TBv=TBv: s.activation(
                    TBv, ZBv, AF.Tanh, scale=cp.get(1.0 / 240.0),
                    bias=t["SBV"][:, h * 64 + sc: h * 64 + sc + 1]),
                    reads=[("ZB", spar), ("SBV", h), "CONST"], writes=[("TB", spar)])
                abs_ = AB[:, apar * 512:(apar + 1) * 512]
                b.emit("scalar", lambda s, abs_=abs_, TBv=TBv: s.activation(
                    abs_, TBv, AF.Exp, scale=30.0, bias=cp.get(-30.0)),
                    reads=[("TB", spar), "CONST"], writes=[("AB", apar)])
                if sc >= 4 * qt:
                    rel = sc - 4 * qt
                    if rel > 0:
                        b.emit("vector", lambda v, abs_=abs_, rel=rel: v.memset(abs_[:, 0:rel * 128], 0.0),
                               reads=[("AB", apar)], writes=[("AB", apar)])
                    b.emit("vector", lambda v, abs_=abs_, rel=rel: v.tensor_tensor(
                        abs_[:, rel * 128:(rel + 1) * 128], abs_[:, rel * 128:(rel + 1) * 128],
                        t["TRIMASK"][:], ALU.mult),
                        reads=[("AB", apar), "TRIMASK"], writes=[("AB", apar)])
                b.emit("tensor", lambda tt, abs_=abs_, sc=sc, YPS=YPS: tt.matmul(
                    YPS[0:65, 0:512], t["VT"][:, sc * 65:(sc + 1) * 65], abs_,
                    start=(sc == 0), stop=(sc == nsc - 1), skip_group_check=True),
                    reads=[("AB", apar), ("VT", sc // 8)], writes=[("psA", h)])
            b.emit("vector", lambda v, YPS=YPS: v.reciprocal(RSB[:], YPS[0:1, 0:512]),
                   reads=[("psA", h)], writes=["RSB"])
            b.emit("tensor", lambda tt: tt.matmul(psB1[0:65, 0:512], t["ONESROW"][0:1, 0:65], RSB[:],
                                                  start=True, stop=True),
                   reads=["RSB", "ONESROW"], writes=["psB1"])
            b.emit("scalar", lambda s: s.copy(RBS[0:65, :], psB1[0:65, 0:512]),
                   reads=["psB1"], writes=["RBS"])
            b.emit("vector", lambda v, YPS=YPS: v.tensor_tensor(YH[0:65, :], YPS[0:65, 0:512],
                                                                RBS[0:65, :], ALU.mult),
                   reads=[("psA", h), "RBS"], writes=["YH"])
            b.dma("gpsimd", "yst", y_fn(h, qt), YH[1:65, :],
                  reads=["YH"], writes=[("yst", qt, h)])


def emit_load_kres1(K, kh_fn):
    b = K.b
    for j in range(8):
        b.dma("sync", "kres", K.t["KRES"][0:64, j * 1024:(j + 1) * 1024],
              kh_fn(j), writes=[("KRES", j)])
    K.b.res["KRES"] = [K.b.res[("KRES", 7)][0], []]


def build_attn1core_launch():
    nc = bass.Bass("TRN2", target_bir_lowering=False)
    kh = nc.dram_tensor("kh", [64, T], BF16, kind="ExternalInput").ap()
    vhf = nc.dram_tensor("vhf", [64, T], BF16, kind="ExternalInput").ap()
    qh = nc.dram_tensor("qh", [256, T], BF16, kind="ExternalInput").ap()
    sbv_in = nc.dram_tensor("sbv", [128, 256], F32, kind="ExternalInput").ap()
    alr_in = nc.dram_tensor("alrt", [1, 4 * T], F32, kind="ExternalInput").ap()
    idn = nc.dram_tensor("idn", [128, 128], BF16, kind="ExternalInput").ap()
    idnf = nc.dram_tensor("idnf", [128, 128], F32, kind="ExternalInput").ap()
    cpd = nc.dram_tensor("cpool", [128, 32], F32, kind="ExternalInput").ap()
    trimask_d = nc.dram_tensor("trimask", [128, 128], BF16, kind="ExternalInput").ap()
    yh = nc.dram_tensor("yh", [256, T], BF16, kind="ExternalOutput").ap()
    es = contextlib.ExitStack()
    cp = ConstPool(nc)
    cp.alloc(es)
    b = Builder(nc)
    K = Kern(nc, b, cp, es)
    with es:
        for i in range(4):
            K.ps("psA%d" % i, [128, 512], F32)
        K.ps("psB0", [128, 512], F32)
        K.ps("psB1", [128, 512], F32)
        K.ps("psC", [128, 512], F32)
        K.ps("psT", [128, 1024], BF16)
        K.sb("ONESB", [128, 1], BF16)
        K.sb("ONESROW", [1, 128], F32)
        K.sb("IDN", [128, 128], BF16)
        K.sb("IDNF", [128, 128], F32)
        b.emit("vector", lambda v: v.memset(K.t["ONESB"][:], 1.0), writes=["ONESB"])
        b.emit("vector", lambda v: v.memset(K.t["ONESROW"][:], 1.0), writes=["ONESROW"])
        alloc_attn_common(K)
        alloc_attn1core(K)
        setup_idn(K, idn, idnf)
        load_const_pool(K, cpd)
        b.dma("sync", "f_tri", K.t["TRIMASK"][:], trimask_d, writes=["TRIMASK"])
        b.dma("sync", "f_sbv", K.t["SBV"][:], sbv_in, writes=[("SBV", 0), ("SBV", 1), ("SBV", 2), ("SBV", 3)])
        b.emit("vector", lambda v: v.memset(K.t["VSTG"][:], 0.0), writes=[("VSTG", 0), ("VSTG", 1)])
        emit_load_kres1(K, lambda j: kh[:, j * 1024:(j + 1) * 1024])
        emit_v_transpose(K, lambda g: vhf[:, g * 1024:(g + 1) * 1024], "VT", 64, aug=True, nrows=64)
        emit_attn1_core_pre(K, lambda h, qt: qh[h * 64:(h + 1) * 64, qt * 512:(qt + 1) * 512],
                            lambda h, qt: yh[h * 64:(h + 1) * 64, qt * 512:(qt + 1) * 512],
                            alr_in)
        b.finalize_stores(["yst"])
        b.build(es)
    return nc, cp


from concourse.aot_env import allow_ici_all_to_all


def emit_a2a(K, in_t, out_t, reads):
    def fn(g):
        with allow_ici_all_to_all():
            return g.collective_compute(
                "AllToAll", ALU.bypass,
                replica_groups=[list(range(NCORE))],
                ins=[in_t.ap().opt()], outs=[out_t.ap().opt()])
    K.b.emit("gpsimd", fn, reads=reads, writes=[("cc", out_t.name)], inc=1, chan="cc")


def build_mega_launch():
    nc = bass.Bass("TRN2", target_bir_lowering=False)

    def ei(name, shape, dt=F32):
        return nc.dram_tensor(name, shape, dt, kind="ExternalInput").ap()
    x_in = ei("x_fm", [D, TL])
    f0wg = ei("f0wg", [D, 5632]); f0wu = ei("f0wu", [D, 5632]); f0dw = ei("f0dw", [5632, D])
    f0gs = ei("f0gs", [1, 5632]); f0us = ei("f0us", [44, 128]); f0ds = ei("f0ds", [16, 128])
    a0wq = ei("a0wq", [D, 2048]); a0wk = ei("a0wk", [D, 1024]); a0wv = ei("a0wv", [D, 1024])
    a0wg = ei("a0wg", [D, 8]); a0wc = ei("a0wc", [D, D]); a0bc = ei("a0bc", [16, 128])
    f1wg = ei("f1wg", [D, 8192]); f1wu = ei("f1wu", [D, 8192]); f1dw = ei("f1dw", [8192, D])
    a1wq = ei("a1wq", [D, 2048]); a1wk = ei("a1wk", [D, 512]); a1wv = ei("a1wv", [D, 512])
    a1wo = ei("a1wo", [D, D]); a1bo = ei("a1bo", [16, 128])
    idn = ei("idn", [128, 128], BF16); idnf = ei("idnf", [128, 128])
    cpd = ei("cpool", [128, 32])
    triu_d = ei("triu", [128, 128]); tris_d = ei("tris", [128, 128])
    trimask_d = ei("trimask", [128, 128], BF16); sel_d = ei("sel", [4, 512])
    cos_d = ei("cosr", [TL, 2048]); sin_d = ei("sinr", [TL, 2048])
    sbv_in = ei("sbv", [128, 256]); alr_in = ei("alrt", [1, 4 * T])
    x_out = nc.dram_tensor("x_out", [D, TL], F32, kind="ExternalOutput").ap()

    def it(name, shape, dt):
        return nc.dram_tensor(name, shape, dt, kind="Internal")
    q0b = it("q0b", [2048, TL], BF16); q0g = it("q0g", [2048, TL], BF16)
    k0b = it("k0b", [1024, TL], BF16); k0g = it("k0g", [1024, TL], BF16)
    v0b = it("v0b", [1024, TL], BF16); v0g = it("v0g", [1024, TL], BF16)
    lg0b = it("lg0b", [8, TL], F32); lg0g = it("lg0g", [8, TL], F32)
    y0b = it("y0b", [2048, TL], BF16); y0g = it("y0g", [2048, TL], BF16)
    q1b = it("q1b", [2048, TL], BF16); q1g = it("q1g", [2048, TL], BF16)
    k1b = it("k1b", [512, TL], BF16); k1g = it("k1g", [512, TL], BF16)
    v1b = it("v1b", [512, TL], BF16); v1g = it("v1g", [512, TL], BF16)
    y1b = it("y1b", [2048, TL], BF16); y1g = it("y1g", [2048, TL], BF16)
    off_d = nc.dram_tensor("offsc", [64], F32, kind="Internal").ap()

    es = contextlib.ExitStack()
    cp = ConstPool(nc)
    cp.alloc(es)
    b = Builder(nc)
    K = Kern(nc, b, cp, es)
    with es:
        alloc_common(K)
        alloc_attn_common(K)
        setup_idn(K, idn, idnf)
        load_const_pool(K, cpd)
        emit_load_x_fm(K, x_in)
        wcfg0 = {"wg": f0wg, "wu": f0wu, "dw": f0dw}
        wcfg1 = {"wg": f1wg, "wu": f1wu, "dw": f1dw}
        with K.stage():
            alloc_ffn(K, 5632)
            alloc_ffn0_scales(K)
            emit_ffn0_scales(K, f0gs, f0us, f0ds)
            for half in (0, 1):
                emit_ffn(K, 0, half, wcfg0)
        with K.stage():
            alloc_proj0(K)
            emit_ln(K, 0, 512, out_col0=0, xln_w=1024)
            emit_ln(K, 512, 512, out_col0=512, xln_w=1024)
            emit_proj0(K, a0wq, a0wk, a0wv, a0wg)
            emit_proj0_out(K, q0b.ap(), k0b.ap(), v0b.ap(), lg0b.ap())
            emit_a2a(K, q0b, q0g, [("po", "q", oc) for oc in range(16)])
            emit_a2a(K, k0b, k0g, [("po", "k", oc) for oc in range(8)])
            emit_a2a(K, v0b, v0g, [("po", "v", oc) for oc in range(8)])
            emit_a2a(K, lg0b, lg0g, [("po", "lg")])
        with K.stage():
            alloc_attn0core(K)
            setup_attn0_consts(K, triu_d, tris_d, trimask_d, sel_d)
            emit_load_kres(K, lambda j: k0g.ap()[128 * j:128 * (j + 1), :])
            emit_v_transpose(K, lambda g: v0g.ap()[128 * g:128 * (g + 1), :], "VT", 128, aug=False)
            emit_lg_cumsum(K, lg0g.ap().rearrange("s (a u) -> (s a) u", u=128), off_d)
            emit_attn0_core(
                K,
                lambda h, qt: q0g.ap()[256 * (qt // 2) + 128 * h: 256 * (qt // 2) + 128 * (h + 1),
                                       (qt % 2) * 512:(qt % 2) * 512 + 512],
                lambda h, qt: y0b.ap()[256 * (qt // 2) + 128 * h: 256 * (qt // 2) + 128 * (h + 1),
                                       (qt % 2) * 512:(qt % 2) * 512 + 512])
            emit_a2a(K, y0b, y0g, [("yst", qt, h) for qt in range(16) for h in range(2)])
        with K.stage():
            alloc_outproj(K)
            emit_load_yt(K, lambda mc: y0g.ap()[mc * 128:(mc + 1) * 128, :])
            emit_bias_load(K, a0bc)
            emit_out_proj(K, a0wc)
        with K.stage():
            alloc_ffn(K, 5632)
            alloc_ffn0_scales(K)
            emit_ffn0_scales(K, f0gs, f0us, f0ds)
            for half in (0, 1):
                emit_ffn(K, 0, half, wcfg0)
        with K.stage():
            alloc_ffn(K, 8192)
            for half in (0, 1):
                emit_ffn(K, 1, half, wcfg1)
        with K.stage():
            alloc_proj1(K)
            emit_ln(K, 0, 512, out_col0=0, xln_w=1024)
            emit_ln(K, 512, 512, out_col0=512, xln_w=1024)
            emit_proj1(K, a1wq, a1wk, a1wv, cos_d, sin_d)
            emit_proj1_out(K, q1b.ap(), k1b.ap(), v1b.ap())
            emit_a2a(K, q1b, q1g, [("po", "q", oc) for oc in range(16)])
            emit_a2a(K, k1b, k1g, [("po", "k", oc) for oc in range(4)])
            emit_a2a(K, v1b, v1g, [("po", "v", oc) for oc in range(4)])
        with K.stage():
            alloc_attn1core(K)
            b.dma("sync", "f_tri", K.t["TRIMASK"][:], trimask_d, writes=["TRIMASK"])
            b.dma("sync", "f_sbv", K.t["SBV"][:], sbv_in,
                  writes=[("SBV", 0), ("SBV", 1), ("SBV", 2), ("SBV", 3)])
            vstg_h = K.t["VSTG"]
            b.emit("vector", lambda v: v.memset(vstg_h[:], 0.0),
                   writes=[("VSTG", 0), ("VSTG", 1)])
            emit_load_kres1(K, lambda j: k1g.ap()[64 * j:64 * (j + 1), :])
            emit_v_transpose(K, lambda g: v1g.ap()[64 * g:64 * (g + 1), :], "VT", 64,
                             aug=True, nrows=64)
            emit_attn1_core_pre(
                K,
                lambda h, qt: q1g.ap()[256 * (qt // 2) + 64 * h: 256 * (qt // 2) + 64 * (h + 1),
                                       (qt % 2) * 512:(qt % 2) * 512 + 512],
                lambda h, qt: y1b.ap()[256 * (qt // 2) + 64 * h: 256 * (qt // 2) + 64 * (h + 1),
                                       (qt % 2) * 512:(qt % 2) * 512 + 512],
                alr_in)
            emit_a2a(K, y1b, y1g, [("yst", qt, h) for qt in range(16) for h in range(4)])
        with K.stage():
            alloc_outproj(K)
            emit_load_yt(K, lambda mc: y1g.ap()[mc * 128:(mc + 1) * 128, :])
            emit_bias_load(K, a1bo)
            emit_out_proj(K, a1wo)
        with K.stage():
            alloc_ffn(K, 8192)
            for half in (0, 1):
                emit_ffn(K, 1, half, wcfg1)
        emit_store_x_fm(K, x_out)
        b.finalize_stores(["xout"])
        b.build(es)
    return nc, cp



_CACHE = {}


def _get(name, builder):
    if name not in _CACHE:
        _CACHE[name] = builder()
    return _CACHE[name]


def _rope_tables_host():
    Dh = 64
    inv = 1.0 / 10000.0 ** (np.arange(0, Dh, 2) / Dh)
    fr = np.arange(T)[:, None] * inv[None, :]
    cos = np.repeat(np.cos(fr), 2, axis=-1)
    sin = np.repeat(np.sin(fr), 2, axis=-1)
    return (np.tile(cos, (1, 32)).astype(np.float32),
            np.tile(sin, (1, 32)).astype(np.float32))


def _run(nc, in_maps):
    return run_bass_kernel_spmd(nc, in_maps, core_ids=list(range(NCORE))).results


def kernel(L0_attn_hidden_states, L0_attn_Wq, L0_attn_Wk, L0_attn_Wv, L0_attn_Wg,
           L0_attn_Wc, L0_attn_bc, L0_ffn_gate_w, L0_ffn_up_w, L0_ffn_down_w,
           L0_ffn_gate_s, L0_ffn_up_s, L0_ffn_down_s, L1_attn_Wq, L1_attn_Wk,
           L1_attn_Wv, L1_attn_Wo, L1_attn_bo, L1_ffn_w_gate, L1_ffn_w_up,
           L1_ffn_w_down):
    x0 = np.asarray(L0_attn_hidden_states)[0]
    idn, idnf = _host_idn(), _host_idnf()
    triu, tris, trimask, sel = _host_triu(), _host_tris(), _host_trimask(), _host_sel()
    cosr, sinr = _rope_tables_host()
    slopes = (2.0 ** (-(np.arange(1, 33) * 8.0 / 32))).astype(np.float64)
    giota = (np.arange(64)[None, :] * 128 + np.arange(128)[:, None]).astype(np.float32)
    tiota = np.arange(T, dtype=np.float64)
    wg0 = np.ascontiguousarray(np.asarray(L0_ffn_gate_w).T)
    wu0 = np.ascontiguousarray(np.asarray(L0_ffn_up_w).T)
    dw0 = np.ascontiguousarray(np.asarray(L0_ffn_down_w).T)
    wg1, wu1, dw1 = np.asarray(L1_ffn_w_gate), np.asarray(L1_ffn_w_up), np.asarray(L1_ffn_w_down)

    nc, cpm = _get("mega", build_mega_launch)
    cpa = cpm.host_array()
    maps = []
    for c in range(NCORE):
        sbv = np.concatenate([giota * np.float32(slopes[4 * c + h] / 240.0) for h in range(4)],
                             axis=1).astype(np.float32)
        alrt = np.concatenate([(-slopes[4 * c + h]) * tiota for h in range(4)]).reshape(1, 4 * T).astype(np.float32)
        maps.append({
            "x_fm": np.ascontiguousarray(x0[c * TL:(c + 1) * TL].T),
            "f0wg": wg0, "f0wu": wu0, "f0dw": dw0,
            "f0gs": np.asarray(L0_ffn_gate_s).reshape(1, -1),
            "f0us": np.asarray(L0_ffn_up_s).reshape(-1, 128),
            "f0ds": np.asarray(L0_ffn_down_s).reshape(-1, 128),
            "a0wq": np.asarray(L0_attn_Wq), "a0wk": np.asarray(L0_attn_Wk),
            "a0wv": np.asarray(L0_attn_Wv), "a0wg": np.asarray(L0_attn_Wg),
            "a0wc": np.asarray(L0_attn_Wc), "a0bc": np.asarray(L0_attn_bc).reshape(16, 128),
            "f1wg": wg1, "f1wu": wu1, "f1dw": dw1,
            "a1wq": np.asarray(L1_attn_Wq), "a1wk": np.asarray(L1_attn_Wk),
            "a1wv": np.asarray(L1_attn_Wv), "a1wo": np.asarray(L1_attn_Wo),
            "a1bo": np.asarray(L1_attn_bo).reshape(16, 128),
            "idn": idn, "idnf": idnf, "cpool": cpa,
            "triu": triu, "tris": tris, "trimask": trimask, "sel": sel,
            "cosr": np.ascontiguousarray(cosr[c * TL:(c + 1) * TL]),
            "sinr": np.ascontiguousarray(sinr[c * TL:(c + 1) * TL]),
            "sbv": sbv, "alrt": alrt,
        })
    r = _run(nc, maps)
    out = np.concatenate([r[c]["x_out"].T for c in range(NCORE)], axis=0)
    return out.reshape(1, T, D).astype(np.float32)


def kernel_unfused(L0_attn_hidden_states, L0_attn_Wq, L0_attn_Wk, L0_attn_Wv, L0_attn_Wg,
                   L0_attn_Wc, L0_attn_bc, L0_ffn_gate_w, L0_ffn_up_w, L0_ffn_down_w,
                   L0_ffn_gate_s, L0_ffn_up_s, L0_ffn_down_s, L1_attn_Wq, L1_attn_Wk,
                   L1_attn_Wv, L1_attn_Wo, L1_attn_bo, L1_ffn_w_gate, L1_ffn_w_up,
                   L1_ffn_w_down):
    x0 = np.asarray(L0_attn_hidden_states)[0]
    idn, idnf = _host_idn(), _host_idnf()
    triu, tris, trimask, sel = _host_triu(), _host_tris(), _host_trimask(), _host_sel()
    cosr, sinr = _rope_tables_host()
    slopes = (2.0 ** (-(np.arange(1, 33) * 8.0 / 32))).astype(np.float64)
    giota = (np.arange(64)[None, :] * 128 + np.arange(128)[:, None]).astype(np.float32)
    tiota = np.arange(T, dtype=np.float64)

    wg0 = np.ascontiguousarray(np.asarray(L0_ffn_gate_w).T)
    wu0 = np.ascontiguousarray(np.asarray(L0_ffn_up_w).T)
    dw0 = np.ascontiguousarray(np.asarray(L0_ffn_down_w).T)
    wg1, wu1, dw1 = np.asarray(L1_ffn_w_gate), np.asarray(L1_ffn_w_up), np.asarray(L1_ffn_w_down)

    nc_f0, cp_f0 = _get("ffn0", lambda: build_ffn_launch(0))
    nc_f1, cp_f1 = _get("ffn1", lambda: build_ffn_launch(1))
    nc_p0, cp_p0 = _get("proj0", build_proj0_launch)
    nc_a0, cp_a0 = _get("attn0", build_attn0core_launch)
    nc_op, cp_op = _get("outproj", build_outproj_launch)
    nc_p1, cp_p1 = _get("proj1", build_proj1_launch)
    nc_a1, cp_a1 = _get("attn1", build_attn1core_launch)

    def ffn0_maps(xs):
        return [{"x_fm": xs[c], "wg": wg0, "wu": wu0, "dw": dw0,
                 "gs": np.asarray(L0_ffn_gate_s).reshape(1, -1),
                 "us": np.asarray(L0_ffn_up_s).reshape(-1, 128),
                 "ds": np.asarray(L0_ffn_down_s).reshape(-1, 128),
                 "idn": idn, "idnf": idnf, "cpool": cp_f0.host_array()} for c in range(NCORE)]

    def ffn1_maps(xs):
        return [{"x_fm": xs[c], "wg": wg1, "wu": wu1, "dw": dw1,
                 "idn": idn, "idnf": idnf, "cpool": cp_f1.host_array()} for c in range(NCORE)]

    xs = [np.ascontiguousarray(x0[c * TL:(c + 1) * TL].T) for c in range(NCORE)]
    r = _run(nc_f0, ffn0_maps(xs))
    xs = [r[c]["x_out"] for c in range(NCORE)]
    r = _run(nc_p0, [{"x_fm": xs[c], "wq": np.asarray(L0_attn_Wq), "wk": np.asarray(L0_attn_Wk),
                      "wv": np.asarray(L0_attn_Wv), "wg": np.asarray(L0_attn_Wg),
                      "idn": idn, "idnf": idnf, "cpool": cp_p0.host_array()} for c in range(NCORE)])
    qf = np.concatenate([r[c]["qf"] for c in range(NCORE)], axis=1)
    kf = np.concatenate([r[c]["kf"] for c in range(NCORE)], axis=1)
    vf = np.concatenate([r[c]["vf"] for c in range(NCORE)], axis=1)
    lg = np.concatenate([r[c]["lg"] for c in range(NCORE)], axis=1)
    r = _run(nc_a0, [{"kh": np.ascontiguousarray(kf[128 * c:128 * (c + 1)]),
                      "vhf": np.ascontiguousarray(vf[128 * c:128 * (c + 1)]),
                      "qh": np.ascontiguousarray(qf[256 * c:256 * (c + 1)]),
                      "lgh": np.ascontiguousarray(lg[c].reshape(64, 128)),
                      "idn": idn, "idnf": idnf, "cpool": cp_a0.host_array(),
                      "triu": triu, "tris": tris, "trimask": trimask, "sel": sel}
                     for c in range(NCORE)])
    y_all = np.concatenate([r[c]["yh"] for c in range(NCORE)], axis=0)
    r = _run(nc_op, [{"x_fm": xs[c], "yt": np.ascontiguousarray(y_all[:, c * TL:(c + 1) * TL]),
                      "w": np.asarray(L0_attn_Wc), "bias": np.asarray(L0_attn_bc).reshape(16, 128),
                      "idn": idn, "idnf": idnf, "cpool": cp_op.host_array()} for c in range(NCORE)])
    xs = [r[c]["x_out"] for c in range(NCORE)]
    r = _run(nc_f0, ffn0_maps(xs))
    xs = [r[c]["x_out"] for c in range(NCORE)]
    r = _run(nc_f1, ffn1_maps(xs))
    xs = [r[c]["x_out"] for c in range(NCORE)]
    r = _run(nc_p1, [{"x_fm": xs[c], "wq": np.asarray(L1_attn_Wq), "wk": np.asarray(L1_attn_Wk),
                      "wv": np.asarray(L1_attn_Wv),
                      "cosr": np.ascontiguousarray(cosr[c * TL:(c + 1) * TL]),
                      "sinr": np.ascontiguousarray(sinr[c * TL:(c + 1) * TL]),
                      "idn": idn, "idnf": idnf, "cpool": cp_p1.host_array()} for c in range(NCORE)])
    qf = np.concatenate([r[c]["qf"] for c in range(NCORE)], axis=1)
    kf = np.concatenate([r[c]["kf"] for c in range(NCORE)], axis=1)
    vf = np.concatenate([r[c]["vf"] for c in range(NCORE)], axis=1)
    maps = []
    for c in range(NCORE):
        sbv = np.concatenate([giota * np.float32(slopes[4 * c + h] / 240.0) for h in range(4)],
                             axis=1).astype(np.float32)
        alrt = np.concatenate([(-slopes[4 * c + h]) * tiota for h in range(4)]).reshape(1, 4 * T).astype(np.float32)
        maps.append({"kh": np.ascontiguousarray(kf[64 * c:64 * (c + 1)]),
                     "vhf": np.ascontiguousarray(vf[64 * c:64 * (c + 1)]),
                     "qh": np.ascontiguousarray(qf[256 * c:256 * (c + 1)]),
                     "sbv": sbv, "alrt": alrt, "idn": idn, "idnf": idnf,
                     "cpool": cp_a1.host_array(), "trimask": trimask})
    r = _run(nc_a1, maps)
    y_all = np.concatenate([r[c]["yh"] for c in range(NCORE)], axis=0)
    r = _run(nc_op, [{"x_fm": xs[c], "yt": np.ascontiguousarray(y_all[:, c * TL:(c + 1) * TL]),
                      "w": np.asarray(L1_attn_Wo), "bias": np.asarray(L1_attn_bo).reshape(16, 128),
                      "idn": idn, "idnf": idnf, "cpool": cp_op.host_array()} for c in range(NCORE)])
    xs = [r[c]["x_out"] for c in range(NCORE)]
    r = _run(nc_f1, ffn1_maps(xs))
    xs = [r[c]["x_out"] for c in range(NCORE)]

    out = np.concatenate([xs[c].T for c in range(NCORE)], axis=0)
    return out.reshape(1, T, D).astype(np.float32)
